# version 05
# baseline (speedup 1.0000x reference)
import numpy as np
from contextlib import ExitStack

import concourse.bass as bass
import concourse.mybir as mybir
from concourse.bass_utils import run_bass_kernel_spmd

F32 = mybir.dt.float32
BF16 = mybir.dt.bfloat16
AF = mybir.ActivationFunctionType
ALU = mybir.AluOpType
AX = mybir.AxisListType

B, S, H = 8, 2048, 2048
NH, HD = 32, 64
NHC = 16
E, I, R = 32, 1408, 128
NIC = 11
SI, NSC = 2816, 22
QT, NQT = 512, 4
NTT = 4
MASKNEG = -60000.0
ATTN_HPS = None
DEBUG_MEMSET_OF = False
DEBUG_HP_BARRIER = False

ENGINES = ("PE", "ACT", "DVE", "POOL", "SP")
ENGMAP = {"PE": "tensor", "ACT": "scalar", "DVE": "vector", "POOL": "gpsimd", "SP": "sync"}


class Res:
    __slots__ = ("name", "writer", "readers")
    def __init__(self, name):
        self.name = name
        self.writer = None
        self.readers = []


class Sched:

    def __init__(self, nc):
        self.nc = nc
        self.sems = {e: nc.alloc_semaphore(f"q_{e}") for e in ENGINES}
        self.count = {e: 0 for e in ENGINES}
        self.ops = {e: [] for e in ENGINES}
        self.clock = {e: {} for e in ENGINES}
        self.postclock = {}
        self.dma_sems = {}
        self.slot_last_pc = {}

    def _resolve(self, eng, events):
        waits = []
        cl = self.clock[eng]
        for ev in events:
            if ev is None:
                continue
            s, v = ev
            pc = None
            if s == "SLOT":
                slot = v
                v = self.dma_sems[slot][1]
                s = slot
                pc = self.slot_last_pc.get(slot)
            else:
                pc = self.postclock.get(ev)
            if eng == "PE" and s == "PE":
                continue
            if cl.get(s, 0) >= v:
                continue
            waits.append((s, v))
            cl[s] = v
            if pc:
                for s2, v2 in pc.items():
                    if cl.get(s2, 0) < v2:
                        cl[s2] = v2
        return waits

    def _events(self, reads, writes):
        need = []
        for r in reads:
            need.append(r.writer)
        for w in writes:
            need.append(w.writer)
            need.extend(w.readers)
        return need

    def _commit(self, ev, reads, writes, eng):
        pc = dict(self.clock[eng])
        pc[ev[0]] = ev[1]
        self.postclock[ev] = pc
        for r in reads:
            r.readers.append(ev)
        for w in writes:
            w.writer = ev
            w.readers = []

    def op(self, eng, fn, reads=(), writes=()):
        waits = self._resolve(eng, self._events(reads, writes))
        self.count[eng] += 1
        ev = (eng, self.count[eng])
        self.ops[eng].append((waits, fn, eng, 1))
        self._commit(ev, reads, writes, eng)
        return ev

    def dma(self, eng, slot, fn, reads=(), writes=()):
        if slot not in self.dma_sems:
            self.dma_sems[slot] = [self.nc.alloc_semaphore(f"d_{slot}"), 0]
        waits = self._resolve(eng, self._events(reads, writes))
        self.dma_sems[slot][1] += 16
        ev = ("SLOT", slot)
        self.ops[eng].append((waits, fn, slot, 16))
        pc = dict(self.clock[eng])
        pc[slot] = self.dma_sems[slot][1]
        self.slot_last_pc[slot] = pc
        for r in reads:
            r.readers.append(ev)
        for w in writes:
            w.writer = ev
            w.readers = []
        return ev

    def hard_barrier(self):
        for eng in ENGINES:
            events = [(e2, self.count[e2]) for e2 in ENGINES if e2 != eng and self.count[e2] > 0]
            events += [("SLOT", s) for s in self.dma_sems]
            waits = self._resolve(eng, events)
            if waits:
                self.ops[eng].append((waits, None, None, 0))

    def final_wait(self, eng):
        events = [(name, c) for name, (h, c) in self.dma_sems.items()]
        waits = self._resolve(eng, events)
        if waits:
            self.ops[eng].append((waits, None, None, 0))

    def emit(self):
        handles = dict(self.sems)
        for name, (h, c) in self.dma_sems.items():
            handles[name] = h
        with self.nc.Block() as block:
            for eng in ENGINES:
                ops = self.ops[eng]
                if not ops:
                    continue
                def body(e, ops=ops):
                    for waits, fn, inc_key, inc_n in ops:
                        for (s, v) in waits:
                            e.wait_ge(handles[s], v)
                        if fn is not None:
                            inst = fn(e)
                            if inc_n:
                                inst.then_inc(handles[inc_key], inc_n)
                getattr(block, ENGMAP[eng])(body)



def _prep_inputs(ins):
    T = lambda a: np.ascontiguousarray(np.asarray(a).T.astype(np.float32, copy=False))
    d = {}
    d["WqT"] = T(ins["L0_attn_Wq"]); d["WkT"] = T(ins["L0_attn_Wk"])
    d["WvT"] = T(ins["L0_attn_Wv"]); d["WoT"] = T(ins["L0_attn_Wo"])
    d["gateT"] = T(ins["L0_ffn_gate_w"])
    d["RgT"] = T(ins["L0_ffn_Rg"]); d["RuT"] = T(ins["L0_ffn_Ru"]); d["RdT"] = T(ins["L0_ffn_Rd"])
    tp = lambda a: np.transpose(np.asarray(a).astype(np.float32, copy=False), (0, 2, 1))
    d["U3T"] = np.ascontiguousarray(np.concatenate(
        [tp(ins["L0_ffn_Ug"]), tp(ins["L0_ffn_Uu"]), tp(ins["L0_ffn_Ud"])], axis=2)).reshape(E * R, 3 * R)
    d["CguT"] = np.ascontiguousarray(np.concatenate(
        [tp(ins["L0_ffn_Cg"]), tp(ins["L0_ffn_Cu"])], axis=2)).reshape(E * R, 2 * I)
    d["CdT"] = np.ascontiguousarray(tp(ins["L0_ffn_Cd"])).reshape(E * R, H)
    d["sRgT"] = T(ins["L0_ffn_s_Rg"]); d["sRuT"] = T(ins["L0_ffn_s_Ru"])
    d["sU3T"] = np.ascontiguousarray(np.concatenate(
        [T(ins["L0_ffn_s_Ug"]), T(ins["L0_ffn_s_Uu"]), T(ins["L0_ffn_s_Ud"])], axis=1))
    d["sCgT"] = T(ins["L0_ffn_s_Cg"]); d["sCuT"] = T(ins["L0_ffn_s_Cu"])
    d["sRdT"] = T(ins["L0_ffn_s_Rd"]); d["sCdT"] = T(ins["L0_ffn_s_Cd"])

    tri = np.tril(np.ones((128, 128), np.float32))
    d["c_trineg"] = -tri
    d["c_ident"] = np.eye(128, dtype=np.float32)
    d["c_ones1"] = np.ones((1, 128), np.float32)
    d["c_negones"] = -np.ones((128, 1), np.float32)
    j = np.arange(128)[:, None]; i = np.arange(QT)[None, :]
    d["c_mneg"] = np.concatenate(
        [(j >= (i - 128 * Dd)).astype(np.float32) * MASKNEG for Dd in range(4)], axis=0)
    d["c_iota32"] = np.broadcast_to(np.arange(E, dtype=np.float32), (128, E)).copy()
    return d


DRAM_SHAPES = {
    "WqT": [H, H], "WkT": [H, H], "WvT": [H, H], "WoT": [H, H],
    "gateT": [H, E], "RgT": [H, R], "RuT": [H, R], "RdT": [I, R],
    "U3T": [E * R, 3 * R], "CguT": [E * R, 2 * I], "CdT": [E * R, H],
    "sRgT": [H, R], "sRuT": [H, R], "sU3T": [R, 3 * R],
    "sCgT": [R, SI], "sCuT": [R, SI], "sRdT": [SI, R], "sCdT": [R, H],
    "c_trineg": [128, 128], "c_ident": [128, 128], "c_ones1": [1, 128],
    "c_negones": [128, 1], "c_mneg": [4 * 128, QT], "c_iota32": [128, E],
}



def build_program(upto="full", taps=()):
    nc = bass.Bass("TRN2", target_bir_lowering=False, debug=False)
    Sc = Sched(nc)

    dx = nc.dram_tensor("x", [S, H], F32, kind="ExternalInput").ap()
    dout = nc.dram_tensor("out", [S, H], F32, kind="ExternalOutput").ap()
    dr = {k: nc.dram_tensor(k, shp, F32, kind="ExternalInput").ap() for k, shp in DRAM_SHAPES.items()}
    d_we = nc.dram_tensor("scr_we", [E, S], F32, kind="Internal").ap()
    d_cgu = nc.dram_tensor("scr_cgu", [E * R, 2 * I], BF16, kind="Internal").ap()
    d_cd = nc.dram_tensor("scr_cd", [E * R, H], BF16, kind="Internal").ap()
    d_u3 = nc.dram_tensor("scr_u3", [E * R, 3 * R], BF16, kind="Internal").ap()

    R_ = {}
    def res(name):
        if name not in R_:
            R_[name] = Res(name)
        return R_[name]

    sbufs = {}
    fence = []
    scope_res = {}
    def sb(name, shape, dtype, scope=None):
        if scope is None:
            t = nc.alloc_sbuf_tensor(name, shape, dtype)
        else:
            t = scope.enter_context(nc.sbuf_tensor(name, shape, dtype))
        sbufs[name] = t
        r = res(name)
        r.readers = list(fence)
        if scope is not None:
            scope_res.setdefault(id(scope), []).append(r)
        return t

    def close_scope(scope):
        for r in scope_res.pop(id(scope), []):
            if r.writer is not None:
                fence.append(r.writer)
            fence.extend(r.readers)
        scope.close()

    pbank = [nc.alloc_psum_tensor(f"pb{i}", [128, 512], F32) for i in range(8)]
    pr = [res(f"pb{i}") for i in range(8)]
    pb_bf = [p.ap().bitcast(BF16) for p in pbank]
    OUT_rb = [res(f"out_rb{i}") for i in range(16)]

    def ldc(dst_ap, src_ap, rname, slot="lconst"):
        Sc.dma("POOL", slot, lambda g, d=dst_ap, s=src_ap: g.dma_start(d, s), writes=[res(rname)])

    t_tri = sb("tri", [128, 128], BF16)
    t_idb = sb("idb", [128, 128], BF16)
    t_idf = sb("idf", [128, 128], F32)
    t_ones1 = sb("ones1", [1, 128], BF16)
    t_ones1f = sb("ones1f", [1, 128], F32)
    t_negones = sb("negones", [128, 1], BF16)
    t_mneg = sb("mneg", [128, 4 * QT], BF16)
    t_iota = sb("iota", [128, E], F32)
    t_rg = sb("rg", [128, S], BF16)
    t_ru = sb("ru", [128, S], BF16)
    t_rgs = sb("rgs", [128, S], BF16)
    t_rus = sb("rus", [128, S], BF16)
    ldc(t_tri[:], dr["c_trineg"], "tri")
    ldc(t_idb[:], dr["c_ident"], "idb")
    ldc(t_idf[:], dr["c_ident"], "idf")
    ldc(t_ones1[:], dr["c_ones1"], "ones1")
    ldc(t_ones1f[:], dr["c_ones1"], "ones1f")
    ldc(t_negones[:], dr["c_negones"], "negones")
    for Dd in range(4):
        ldc(t_mneg[:, Dd * QT:(Dd + 1) * QT], dr["c_mneg"][Dd * 128:(Dd + 1) * 128, :], "mneg")
    ldc(t_iota[:], dr["c_iota32"], "iota")

    for e in range(E):
        Sc.dma("POOL", "scast", lambda g, e=e: g.dma_start(d_cgu[e * R:(e + 1) * R, :], dr["CguT"][e * R:(e + 1) * R, :]),
               writes=[res(f"scr_cgu{e}")])
        Sc.dma("POOL", "scast", lambda g, e=e: g.dma_start(d_cd[e * R:(e + 1) * R, :], dr["CdT"][e * R:(e + 1) * R, :]),
               writes=[res(f"scr_cd{e}")])
        Sc.dma("POOL", "scast", lambda g, e=e: g.dma_start(d_u3[e * R:(e + 1) * R, :], dr["U3T"][e * R:(e + 1) * R, :]),
               writes=[res(f"scr_u3{e}")])

    s_oF = ExitStack()
    t_oF = sb("oF", [128, NHC * S], BF16, s_oF)
    if DEBUG_MEMSET_OF:
        Sc.op("DVE", lambda v: v.memset(t_oF[:], 0.0), writes=[res("oF")])

    s1 = ExitStack()
    t_xnF = sb("xnF", [128, NHC * S], BF16, s1)

    s1a = ExitStack()
    t_x = [sb(f"x{i}", [128, H], F32, s1a) for i in range(3)]
    t_xn = [sb(f"xn{i}", [128, H], BF16, s1a) for i in range(2)]
    t_stat2 = [sb(f"stat{i}", [128, 8], F32, s1a) for i in range(2)]
    t_xsq2 = [sb(f"xsq{i}", [128, H], F32, s1a) for i in range(2)]
    rxnF = res("xnF")

    for tt in range(16):
        t_stat = t_stat2[tt % 2]; t_xsq = t_xsq2[tt % 2]
        rstat, rxsq = res(f"stat{tt % 2}"), res(f"xsq{tt % 2}")
        xb = t_x[tt % 3]; rxb = res(f"x{tt % 3}")
        Sc.dma("SP", f"lx{tt % 3}", lambda s, d=xb, r0=tt * 128: s.dma_start(d[:], dx[r0:r0 + 128, :]), writes=[rxb])
        st = t_stat; xq = t_xsq
        Sc.op("DVE", lambda v, xb=xb, st=st: v.tensor_reduce(st[:, 0:1], xb[:], axis=AX.X, op=ALU.add),
              reads=[rxb], writes=[rstat])
        Sc.op("DVE", lambda v, xb=xb, xq=xq: v.tensor_tensor(xq[:], xb[:], xb[:], op=ALU.mult),
              reads=[rxb], writes=[rxsq])
        Sc.op("DVE", lambda v, st=st, xq=xq: v.tensor_reduce(st[:, 1:2], xq[:], axis=AX.X, op=ALU.add),
              reads=[rxsq], writes=[rstat])
        Sc.op("DVE", lambda v, st=st: v.tensor_scalar(st[:, 2:4], st[:, 0:2], 1.0 / H, None, op0=ALU.mult),
              reads=[rstat], writes=[rstat])
        Sc.op("DVE", lambda v, st=st: v.tensor_tensor(st[:, 4:5], st[:, 2:3], st[:, 2:3], op=ALU.mult),
              reads=[rstat], writes=[rstat])
        Sc.op("DVE", lambda v, st=st: v.tensor_tensor(st[:, 5:6], st[:, 3:4], st[:, 4:5], op=ALU.subtract),
              reads=[rstat], writes=[rstat])
        Sc.op("DVE", lambda v, st=st: v.tensor_scalar(st[:, 5:6], st[:, 5:6], 1e-5, None, op0=ALU.add),
              reads=[rstat], writes=[rstat])
        Sc.op("ACT", lambda sc, st=st: sc.activation(st[:, 6:7], st[:, 5:6], AF.Ln),
              reads=[rstat], writes=[rstat])
        Sc.op("ACT", lambda sc, st=st: sc.activation(st[:, 7:8], st[:, 6:7], AF.Exp, scale=-0.5),
              reads=[rstat], writes=[rstat])
        xnb = t_xn[tt % 2]; rxn = res(f"xn{tt % 2}")
        Sc.op("DVE", lambda v, xb=xb, xnb=xnb, st=st: v.tensor_scalar(
            xnb[:], xb[:], st[:, 2:3], st[:, 7:8], op0=ALU.subtract, op1=ALU.mult),
            reads=[rxb, rstat], writes=[rxn])
        for hc in range(NHC):
            bk = hc % 2
            Sc.op("PE", lambda t, xnb=xnb, hc=hc, bk=bk: t.transpose(
                pb_bf[bk][:, 0:128], xnb[:, hc * 128:(hc + 1) * 128], t_idb[:]),
                reads=[rxn, res("idb")], writes=[pr[bk]])
            Sc.op("DVE", lambda v, hc=hc, tt=tt, bk=bk: v.tensor_copy(
                t_xnF[:, hc * S + tt * 128: hc * S + tt * 128 + 128], pb_bf[bk][:, 0:128]),
                reads=[pr[bk]], writes=[rxnF])
    close_scope(s1a)

    s1b = ExitStack()
    t_rw = sb("rw", [128, NHC * 128], BF16, s1b)
    t_gateT = sb("gateTs", [128, NHC * E], BF16, s1b)
    t_lgF2 = [sb(f"lgF{i}", [32, QT], F32, s1b) for i in range(2)]
    t_lgT2 = [sb(f"lgT{i}", [128, E], F32, s1b) for i in range(2)]
    t_tw2 = [sb(f"tw{i}", [128, 8], F32, s1b) for i in range(2)]
    t_twi2 = [sb(f"twi{i}", [128, 8], mybir.dt.uint32, s1b) for i in range(2)]
    t_twif2 = [sb(f"twif{i}", [128, 8], F32, s1b) for i in range(2)]
    t_wnum2 = [sb(f"wnum{i}", [128, 8], F32, s1b) for i in range(2)]
    t_wden2 = [sb(f"wden{i}", [128, 2], F32, s1b) for i in range(2)]
    t_wacc2 = [sb(f"wacc{i}", [128, 4 * E], F32, s1b) for i in range(2)]
    t_weT = sb("weT", [32, S], F32, s1b)

    def proj128(dst, rdst, wname):
        rw = res("rw")
        for hc in range(NHC):
            ldc(t_rw[:, hc * 128:(hc + 1) * 128], dr[wname][hc * 128:(hc + 1) * 128, :], "rw", slot=f"lrw")
        for nt in range(NQT):
            bk = 4 + (nt % 2)
            for hc in range(NHC):
                Sc.op("PE", lambda t, hc=hc, nt=nt, bk=bk: t.matmul(
                    pbank[bk][:, 0:QT], t_rw[:, hc * 128:(hc + 1) * 128],
                    t_xnF[:, hc * S + nt * QT: hc * S + (nt + 1) * QT],
                    start=(hc == 0), stop=True, skip_group_check=(hc != 0)),
                    reads=[rw, rxnF], writes=[pr[bk]])
            Sc.op("DVE", lambda v, nt=nt, bk=bk: v.tensor_copy(
                dst[:, nt * QT:(nt + 1) * QT], pbank[bk][:, 0:QT]),
                reads=[pr[bk]], writes=[rdst])

    for dst, nm, wname in [(t_rg, "rg", "RgT"), (t_ru, "ru", "RuT"),
                           (t_rgs, "rgs", "sRgT"), (t_rus, "rus", "sRuT")]:
        proj128(dst, res(nm), wname)

    for hc in range(NHC):
        ldc(t_gateT[:, hc * E:(hc + 1) * E], dr["gateT"][hc * 128:(hc + 1) * 128, :], "gateTs", slot="lg")
    rweT = res("weT")
    for nt in range(NQT):
        gp = nt % 2
        t_lgF = t_lgF2[gp]; rlgF = res(f"lgF{gp}")
        gbk = 6 + gp
        for hc in range(NHC):
            Sc.op("PE", lambda t, hc=hc, nt=nt, gbk=gbk: t.matmul(
                pbank[gbk][0:32, 0:QT], t_gateT[:, hc * E:(hc + 1) * E],
                t_xnF[:, hc * S + nt * QT: hc * S + (nt + 1) * QT],
                start=(hc == 0), stop=True, skip_group_check=(hc != 0)),
                reads=[res("gateTs"), rxnF], writes=[pr[gbk]])
        Sc.op("DVE", lambda v, t_lgF=t_lgF, gbk=gbk: v.tensor_copy(t_lgF[:], pbank[gbk][0:32, 0:QT]),
              reads=[pr[gbk]], writes=[rlgF])
        for tc in range(4):
            tok0 = nt * QT + tc * 128
            p = tc % 2
            t_lgT = t_lgT2[p]; t_tw = t_tw2[p]; t_twi = t_twi2[p]; t_twif = t_twif2[p]
            t_wnum = t_wnum2[p]; t_wden = t_wden2[p]; t_wacc = t_wacc2[p]
            rlgT, rtw, rtwi, rtwif = res(f"lgT{p}"), res(f"tw{p}"), res(f"twi{p}"), res(f"twif{p}")
            rwnum, rwden, rwacc = res(f"wnum{p}"), res(f"wden{p}"), res(f"wacc{p}")
            tbk = 4 + p
            Sc.op("PE", lambda t, tc=tc, t_lgF=t_lgF, tbk=tbk: t.transpose(
                pbank[tbk][0:128, 0:32], t_lgF[0:32, tc * 128:(tc + 1) * 128], t_idf[0:32, 0:32]),
                reads=[rlgF, res("idf")], writes=[pr[tbk]])
            Sc.op("DVE", lambda v, t_lgT=t_lgT, tbk=tbk: v.tensor_copy(t_lgT[:], pbank[tbk][0:128, 0:32]),
                  reads=[pr[tbk]], writes=[rlgT])
            Sc.op("DVE", lambda v, t_tw=t_tw, t_lgT=t_lgT: v.max(t_tw[:], t_lgT[:]),
                  reads=[rlgT], writes=[rtw])
            Sc.op("DVE", lambda v, t_twi=t_twi, t_tw=t_tw, t_lgT=t_lgT: v.max_index(t_twi[:], t_tw[:], t_lgT[:]),
                  reads=[rtw, rlgT], writes=[rtwi])
            Sc.op("DVE", lambda v, t_twif=t_twif, t_twi=t_twi: v.tensor_copy(t_twif[:], t_twi[:]),
                  reads=[rtwi], writes=[rtwif])
            Sc.op("ACT", lambda sc, t_wnum=t_wnum, t_tw=t_tw: sc.activation(t_wnum[:, 0:4], t_tw[:, 0:4], AF.Exp),
                  reads=[rtw], writes=[rwnum])
            Sc.op("DVE", lambda v, t_wden=t_wden, t_wnum=t_wnum: v.tensor_reduce(t_wden[:, 0:1], t_wnum[:, 0:4], axis=AX.X, op=ALU.add),
                  reads=[rwnum], writes=[rwden])
            Sc.op("DVE", lambda v, t_wden=t_wden: v.reciprocal(t_wden[:, 1:2], t_wden[:, 0:1]),
                  reads=[rwden], writes=[rwden])
            Sc.op("DVE", lambda v, t_wnum=t_wnum, t_wden=t_wden: v.tensor_scalar(t_wnum[:, 4:8], t_wnum[:, 0:4], t_wden[:, 1:2], None, op0=ALU.mult),
                  reads=[rwden, rwnum], writes=[rwnum])
            for k in range(4):
                Sc.op("DVE", lambda v, k=k, t_wacc=t_wacc, t_twif=t_twif, t_wnum=t_wnum: v.tensor_scalar(
                    t_wacc[:, k * E:(k + 1) * E], t_iota[:], t_twif[:, k:k + 1], t_wnum[:, 4 + k:5 + k],
                    op0=ALU.is_equal, op1=ALU.mult),
                    reads=[res("iota"), rtwif, rwnum], writes=[rwacc])
            Sc.op("DVE", lambda v, t_wacc=t_wacc: v.tensor_tensor(t_wacc[:, 0:E], t_wacc[:, 0:E], t_wacc[:, E:2 * E], op=ALU.add),
                  reads=[rwacc], writes=[rwacc])
            Sc.op("DVE", lambda v, t_wacc=t_wacc: v.tensor_tensor(t_wacc[:, 2 * E:3 * E], t_wacc[:, 2 * E:3 * E], t_wacc[:, 3 * E:4 * E], op=ALU.add),
                  reads=[rwacc], writes=[rwacc])
            Sc.op("DVE", lambda v, t_wacc=t_wacc: v.tensor_tensor(t_wacc[:, 0:E], t_wacc[:, 0:E], t_wacc[:, 2 * E:3 * E], op=ALU.add),
                  reads=[rwacc], writes=[rwacc])
            Sc.op("PE", lambda t, t_wacc=t_wacc, tbk=tbk: t.transpose(pbank[tbk][0:32, 0:128], t_wacc[:, 0:E], t_idf[:]),
                  reads=[rwacc, res("idf")], writes=[pr[tbk]])
            Sc.op("DVE", lambda v, tok0=tok0, tbk=tbk: v.tensor_copy(t_weT[:, tok0:tok0 + 128], pbank[tbk][0:32, 0:128]),
                  reads=[pr[tbk]], writes=[rweT])
    Sc.dma("SP", "swe", lambda s: s.dma_start(d_we, t_weT[:]), reads=[rweT], writes=[res("scr_we")])

    taps_out = {}
    def do_taps(names):
        for nm in names:
            if nm in sbufs and nm not in taps_out:
                t = sbufs[nm]
                shp = list(t.ap().shape)
                dt_ = nc.dram_tensor(f"dbg_{nm}", shp, t.ap().dtype, kind="ExternalOutput").ap()
                Sc.dma("SP", f"dbg_{nm}", lambda s, dt_=dt_, t=t: s.dma_start(dt_, t[:]), reads=[res(nm)])
                taps_out[nm] = True

    if upto == "phase0":
        do_taps(taps)
        close_scope(s1b)
        Sc.final_wait("SP")
        Sc.emit()
        return nc
    close_scope(s1b)

    s2 = ExitStack()
    t_wq = sb("wq", [128, NHC * 128], BF16, s2)
    t_wk = sb("wk", [128, NHC * 128], BF16, s2)
    t_wv = sb("wv", [128, NHC * 128], BF16, s2)
    t_q2 = [sb(f"q{i}", [128, S], BF16, s2) for i in range(2)]
    t_k2 = [sb(f"k{i}", [128, S], BF16, s2) for i in range(2)]
    t_vtok2 = [sb(f"vtok{i}", [128, NHC * 128], BF16, s2) for i in range(2)]
    t_vF = sb("vF", [128, S], BF16, s2)
    t_texp = [sb(f"texp{i}", [128, QT], F32, s2) for i in range(2)]
    t_sp = [sb(f"sp{i}", [128, QT], BF16, s2) for i in range(4)]
    t_att = [sb(f"att{i}", [128, QT], BF16, s2) for i in range(4)]
    t_lsb = [sb(f"lsb{i}", [1, QT], BF16, s2) for i in range(2)]

    ZB = {0: (0, 1), 1: (6, 7)}
    PO_ = {0: 3, 1: 5}
    PL = 2
    QKVB = (4, 4)

    def gen_qkv(hp):
        par = hp % 2
        t_q, t_k, t_vtok = t_q2[par], t_k2[par], t_vtok2[par]
        rq, rk, rvtok = res(f"q{par}"), res(f"k{par}"), res(f"vtok{par}")
        for wname, rn in [("WqT", "wq"), ("WkT", "wk"), ("WvT", "wv")]:
            tdst = {"wq": t_wq, "wk": t_wk, "wv": t_wv}[rn]
            for hc in range(NHC):
                Sc.dma("POOL", f"l{rn}", lambda g, hc=hc, wname=wname, tdst=tdst, hp=hp: g.dma_start(
                    tdst[:, hc * 128:(hc + 1) * 128],
                    dr[wname][hc * 128:(hc + 1) * 128, hp * 128:(hp + 1) * 128]),
                    writes=[res(rn)])
            yield
        for tdst, rn, rdst, scale, wsrc, rwn in [
                (t_q, "q", rq, 0.125, t_wq, "wq"),
                (t_k, "k", rk, None, t_wk, "wk"),
                (t_vF, "vF", res("vF"), None, t_wv, "wv")]:
            rwsrc = res(rwn)
            for nt in range(NQT):
                bk = 4
                for hc in range(NHC):
                    Sc.op("PE", lambda t, hc=hc, nt=nt, bk=bk, wsrc=wsrc: t.matmul(
                        pbank[bk][:, 0:QT], wsrc[:, hc * 128:(hc + 1) * 128],
                        t_xnF[:, hc * S + nt * QT: hc * S + (nt + 1) * QT],
                        start=(hc == 0), stop=True, skip_group_check=(hc != 0)),
                        reads=[rwsrc, rxnF], writes=[pr[bk]])
                    if hc % 4 == 3:
                        yield
                if scale is None:
                    Sc.op("DVE", lambda v, nt=nt, bk=bk, tdst=tdst: v.tensor_copy(
                        tdst[:, nt * QT:(nt + 1) * QT], pbank[bk][:, 0:QT]),
                        reads=[pr[bk]], writes=[rdst])
                else:
                    Sc.op("DVE", lambda v, nt=nt, bk=bk, tdst=tdst, scale=scale: v.tensor_scalar(
                        tdst[:, nt * QT:(nt + 1) * QT], pbank[bk][:, 0:QT], scale, None, op0=ALU.mult),
                        reads=[pr[bk]], writes=[rdst])
                yield
        for kc in range(NHC):
            bk = 4
            Sc.op("PE", lambda t, kc=kc, bk=bk: t.transpose(
                pb_bf[bk][:, 0:128], t_vF[:, kc * 128:(kc + 1) * 128], t_idb[:]),
                reads=[res("vF"), res("idb")], writes=[pr[bk]])
            Sc.op("DVE", lambda v, kc=kc, bk=bk, t_vtok=t_vtok: v.tensor_copy(
                t_vtok[:, kc * 128:(kc + 1) * 128], pb_bf[bk][:, 0:128]),
                reads=[pr[bk]], writes=[rvtok])
            if kc % 2 == 1:
                yield

    def gen_head(hp, hs):
        par = hp % 2
        zb = ZB[hs]
        PO = PO_[hs]
        p0 = 64 * hs
        t_q, t_k, t_vtok = t_q2[par], t_k2[par], t_vtok2[par]
        rq, rk, rvtok = res(f"q{par}"), res(f"k{par}"), res(f"vtok{par}")
        lsb = t_lsb[hs]
        rlsb = res(f"lsb{hs}")
        sbase = 2 * hs

        def emit_z(jb, qt):
            pz = zb[jb % 2]
            Dd = jb - 4 * qt
            Sc.op("PE", lambda t, jb=jb, qt=qt, p0=p0, pz=pz, t_k=t_k, t_q=t_q: t.matmul(
                pbank[pz][:, 0:QT],
                t_k[p0:p0 + 64, jb * 128:(jb + 1) * 128],
                t_q[p0:p0 + 64, qt * QT:(qt + 1) * QT],
                start=True, stop=True),
                reads=[rk, rq], writes=[pr[pz]])
            if Dd >= 0:
                Sc.op("PE", lambda t, Dd=Dd, pz=pz: t.matmul(
                    pbank[pz][:, 0:QT], t_idb[:],
                    t_mneg[:, Dd * QT:(Dd + 1) * QT],
                    start=False, stop=True, skip_group_check=True),
                    reads=[res("idb"), res("mneg")], writes=[pr[pz]])

        def emit_att_o(pjb, jbmax):
            ppz, psl = zb[pjb % 2], sbase + pjb % 2
            Sc.op("ACT", lambda sc, psl=psl, ppz=ppz: sc.activation(
                t_att[psl][:], pbank[ppz][:], AF.Exp),
                reads=[pr[ppz]], writes=[res(f"att{psl}")])
            Sc.op("PE", lambda t, psl=psl, pjb=pjb, p0=p0, first=(pjb == jbmax), t_vtok=t_vtok, PO=PO: t.matmul(
                pbank[PO][0:64, 0:QT],
                t_vtok[:, pjb * 128 + p0: pjb * 128 + p0 + 64],
                t_att[psl][:],
                start=first, stop=True, skip_group_check=not first),
                reads=[rvtok, res(f"att{psl}")], writes=[pr[PO]])

        for qt in range(NQT):
            jbmax = 4 * qt + 3
            Sc.op("DVE", lambda v, lsb=lsb: v.memset(lsb[:], 0.0), writes=[rlsb])
            emit_z(jbmax, qt)
            pend = None
            for jb in range(jbmax, -1, -1):
                pz = zb[jb % 2]
                sl = sbase + jb % 2
                tx = jb % 2
                Sc.op("ACT", lambda sc, tx=tx, pz=pz: sc.activation(
                    t_texp[tx][:], pbank[pz][:], AF.Exp),
                    reads=[pr[pz]], writes=[res(f"texp{tx}")])
                Sc.op("ACT", lambda sc, sl=sl, tx=tx: sc.activation(
                    t_sp[sl][:], t_texp[tx][:], AF.Ln, bias=1.0),
                    reads=[res(f"texp{tx}")], writes=[res(f"sp{sl}")])
                yield
                if pend is not None:
                    emit_att_o(pend, jbmax)
                if jb > 0:
                    emit_z(jb - 1, qt)
                yield
                Sc.op("PE", lambda t, sl=sl, pz=pz: t.matmul(
                    pbank[pz][:, 0:QT], t_tri[:], t_sp[sl][:],
                    start=False, stop=True, skip_group_check=True),
                    reads=[res("tri"), res(f"sp{sl}")], writes=[pr[pz]])
                if jb != jbmax:
                    Sc.op("PE", lambda t, pz=pz, lsb=lsb: t.matmul(
                        pbank[pz][:, 0:QT], t_ones1[:], lsb[:],
                        start=False, stop=True, skip_group_check=True),
                        reads=[res("ones1"), rlsb], writes=[pr[pz]])
                if jb != 0:
                    Sc.op("PE", lambda t, sl=sl: t.matmul(
                        pbank[PL][0:1, 0:QT], t_negones[:], t_sp[sl][:],
                        start=True, stop=True),
                        reads=[res("negones"), res(f"sp{sl}")], writes=[pr[PL]])
                    Sc.op("DVE", lambda v, lsb=lsb: v.tensor_tensor(
                        lsb[:], lsb[:], pbank[PL][0:1, 0:QT], op=ALU.add),
                        reads=[pr[PL], rlsb], writes=[rlsb])
                yield
                pend = jb
            emit_att_o(pend, jbmax)
            Sc.op("DVE", lambda v, p0=p0, qt=qt, hp=hp, PO=PO: v.tensor_copy(
                t_oF[p0:p0 + 64, hp * S + qt * QT: hp * S + (qt + 1) * QT],
                pbank[PO][0:64, 0:QT]),
                reads=[pr[PO]], writes=[res("oF")])
            yield

    hps = list(range(NHC)) if ATTN_HPS is None else list(ATTN_HPS)
    qkv_units = [gen_qkv(hp) for hp in hps]
    for _ in qkv_units[0]:
        pass
    for idx, hp in enumerate(hps):
        nxt = qkv_units[idx + 1] if idx + 1 < len(hps) else None
        gens = [gen_head(hp, 0), gen_head(hp, 1)]
        step = 0
        while gens:
            for gh in list(gens):
                try:
                    next(gh)
                except StopIteration:
                    gens.remove(gh)
            step += 1
            if nxt is not None and step % 2 == 0:
                try:
                    next(nxt)
                except StopIteration:
                    nxt = None
        while nxt is not None:
            try:
                next(nxt)
            except StopIteration:
                nxt = None

    close_scope(s2)
    close_scope(s1)

    if upto == "attn":
        do_taps(taps)
        Sc.final_wait("SP")
        Sc.emit()
        return nc

    s3 = ExitStack()
    t_wo = sb("wo", [128, NHC * QT], BF16, s3)
    t_xp = [sb(f"xp{i}", [128, QT], F32, s3) for i in range(2)]
    t_outp = [sb(f"op{i}", [128, QT], F32, s3) for i in range(3)]
    roF = res("oF")
    for ft in range(4):
        for hc in range(NHC):
            Sc.dma("POOL", "lwo", lambda g, hc=hc, ft=ft: g.dma_start(
                t_wo[:, hc * QT:(hc + 1) * QT],
                dr["WoT"][hc * 128:(hc + 1) * 128, ft * QT:(ft + 1) * QT]),
                writes=[res("wo")])
        for tc in range(16):
            bk = tc % 2
            for hc in range(NHC):
                Sc.op("PE", lambda t, hc=hc, tc=tc, ft=ft, bk=bk: t.matmul(
                    pbank[bk][:, 0:QT],
                    t_oF[:, hc * S + tc * 128: hc * S + tc * 128 + 128],
                    t_wo[:, hc * QT:(hc + 1) * QT],
                    start=(hc == 0), stop=True, skip_group_check=(hc != 0)),
                    reads=[roF, res("wo")], writes=[pr[bk]])
            xs = tc % 2
            rxp = res(f"xp{xs}")
            Sc.dma("SP", f"lxp{xs}", lambda s, tc=tc, ft=ft, xs=xs: s.dma_start(
                t_xp[xs][:], dx[tc * 128:(tc + 1) * 128, ft * QT:(ft + 1) * QT]),
                writes=[rxp])
            osl = tc % 3
            rop = res(f"op{osl}")
            Sc.op("DVE", lambda v, bk=bk, xs=xs, osl=osl: v.tensor_tensor(
                t_outp[osl][:], pbank[bk][:, 0:QT], t_xp[xs][:], op=ALU.add),
                reads=[pr[bk], rxp], writes=[rop])
            Sc.dma("SP", "sout", lambda s, tc=tc, ft=ft, osl=osl: s.dma_start(
                dout[tc * 128:(tc + 1) * 128, ft * QT:(ft + 1) * QT], t_outp[osl][:]),
                reads=[rop], writes=[OUT_rb[tc]])
    close_scope(s3)
    close_scope(s_oF)

    s4 = ExitStack()
    t_rdT = sb("rdT", [128, NIC * 128], BF16, s4)
    t_su3 = sb("su3", [128, 3 * R], BF16, s4)
    t_scg = sb("scg", [128, SI], BF16, s4)
    t_scu = sb("scu", [128, SI], BF16, s4)
    t_srd = sb("srd", [128, NSC * 128], BF16, s4)
    t_scd = sb("scd", [128, H], BF16, s4)
    for c in range(NIC):
        ldc(t_rdT[:, c * 128:(c + 1) * 128], dr["RdT"][c * 128:(c + 1) * 128, :], "rdT", slot="lrdT")
    ldc(t_su3[:], dr["sU3T"], "su3", slot="lsh")
    ldc(t_scg[:], dr["sCgT"], "scg", slot="lsh")
    ldc(t_scu[:], dr["sCuT"], "scu", slot="lsh")
    for c in range(NSC):
        ldc(t_srd[:, c * 128:(c + 1) * 128], dr["sRdT"][c * 128:(c + 1) * 128, :], "srd", slot="lsh")
    ldc(t_scd[:], dr["sCdT"], "scd", slot="lsh")

    t_cgu = [sb(f"cgu{i}", [128, 2 * I], BF16, s4) for i in range(4)]
    t_u3 = [sb(f"u3_{i}", [128, 3 * R], BF16, s4) for i in range(4)]
    t_cd = [sb(f"cd{i}", [128, H], BF16, s4) for i in range(3)]
    t_zg = [sb(f"zg{i}", [128, QT], BF16, s4) for i in range(2)]
    t_zu = [sb(f"zu{i}", [128, QT], BF16, s4) for i in range(2)]
    t_g = [sb(f"g{i}", [128, NIC * QT], BF16, s4) for i in range(2)]
    t_gu = [sb(f"gu{i}", [128, NIC * QT], BF16, s4) for i in range(2)]
    t_wb = [sb(f"wb{i}", [128, QT], BF16, s4) for i in range(2)]
    t_rds = [sb(f"rds{i}", [128, QT], BF16, s4) for i in range(2)]
    t_zd = sb("zd", [128, (E + 1) * QT], BF16, s4)
    t_werow = [sb(f"werow{i}", [1, QT], F32, s4) for i in range(4)]
    t_yst = [sb(f"yst{i}", [128, H], F32, s4) for i in range(2)]
    rzd = res("zd")

    def emit_eloads(e, tt):
        cur = e % 4
        Sc.dma("POOL", f"lcgu{cur}", lambda gq, e=e, cur=cur: gq.dma_start(
            t_cgu[cur][:], d_cgu[e * R:(e + 1) * R, :]), reads=[res(f"scr_cgu{e}")], writes=[res(f"cgu{cur}")])
        Sc.dma("POOL", f"lu3{cur}", lambda gq, e=e, cur=cur: gq.dma_start(
            t_u3[cur][:], d_u3[e * R:(e + 1) * R, :]), reads=[res(f"scr_u3{e}")], writes=[res(f"u3_{cur}")])
        Sc.dma("SP", f"lwer{cur}", lambda s, e=e, cur=cur, tt=tt: s.dma_start(
            t_werow[cur][:], d_we[e:e + 1, tt * QT:(tt + 1) * QT]),
            reads=[res("scr_we")], writes=[res(f"werow{cur}")])

    def gen_expert(e, tt, par):
        b0, b1, b2, b3 = 4 * par, 4 * par + 1, 4 * par + 2, 4 * par + 3
        zg, zu, g, gu, wb, rds = t_zg[par], t_zu[par], t_g[par], t_gu[par], t_wb[par], t_rds[par]
        rzg, rzu, rg_, rgu = res(f"zg{par}"), res(f"zu{par}"), res(f"g{par}"), res(f"gu{par}")
        rwb, rrds = res(f"wb{par}"), res(f"rds{par}")
        cur = e % 4
        rcgu, ru3 = res(f"cgu{cur}"), res(f"u3_{cur}")
        if e + 2 < E:
            emit_eloads(e + 2, tt)
        yield
        Sc.op("PE", lambda t, cur=cur, tt=tt: t.matmul(
            pbank[b0][:, 0:QT], t_u3[cur][:, 0:128], t_rg[:, tt * QT:(tt + 1) * QT],
            start=True, stop=True), reads=[ru3, res("rg")], writes=[pr[b0]])
        Sc.op("ACT", lambda sc, zg=zg: sc.activation(zg[:], pbank[b0][:, 0:QT], AF.Copy),
              reads=[pr[b0]], writes=[rzg])
        Sc.op("PE", lambda t, cur=cur, tt=tt: t.matmul(
            pbank[b0][:, 0:QT], t_u3[cur][:, 128:256], t_ru[:, tt * QT:(tt + 1) * QT],
            start=True, stop=True), reads=[ru3, res("ru")], writes=[pr[b0]])
        Sc.op("ACT", lambda sc, zu=zu: sc.activation(zu[:], pbank[b0][:, 0:QT], AF.Copy),
              reads=[pr[b0]], writes=[rzu])
        Sc.op("PE", lambda t, cur=cur: t.matmul(
            pbank[b0][:, 0:QT], t_ones1f[:], t_werow[cur][:],
            start=True, stop=True), reads=[res("ones1f"), res(f"werow{cur}")], writes=[pr[b0]])
        Sc.op("DVE", lambda v, wb=wb: v.tensor_copy(wb[:], pbank[b0][:, 0:QT]),
              reads=[pr[b0]], writes=[rwb])
        yield
        for c in range(NIC):
            Sc.op("PE", lambda t, c=c, cur=cur, zg=zg: t.matmul(
                pbank[b1][:, 0:QT], t_cgu[cur][:, c * 128:(c + 1) * 128], zg[:],
                start=True, stop=True), reads=[rcgu, rzg], writes=[pr[b1]])
            Sc.op("PE", lambda t, c=c, cur=cur, zu=zu: t.matmul(
                pbank[b2][:, 0:QT], t_cgu[cur][:, I + c * 128: I + (c + 1) * 128], zu[:],
                start=True, stop=True), reads=[rcgu, rzu], writes=[pr[b2]])
            Sc.op("ACT", lambda sc, c=c, g=g: sc.activation(
                g[:, c * QT:(c + 1) * QT], pbank[b1][:, 0:QT], AF.Silu),
                reads=[pr[b1]], writes=[rg_])
            if c > 0:
                Sc.op("PE", lambda t, c=c, gu=gu: t.matmul(
                    pbank[b3][:, 0:QT], t_rdT[:, (c - 1) * 128:c * 128], gu[:, (c - 1) * QT:c * QT],
                    start=(c == 1), stop=True, skip_group_check=(c != 1)),
                    reads=[res("rdT"), rgu], writes=[pr[b3]])
            Sc.op("DVE", lambda v, c=c, g=g, gu=gu: v.tensor_tensor(
                gu[:, c * QT:(c + 1) * QT], g[:, c * QT:(c + 1) * QT], pbank[b2][:, 0:QT],
                op=ALU.mult), reads=[rg_, pr[b2]], writes=[rgu])
            yield
        Sc.op("PE", lambda t, gu=gu: t.matmul(
            pbank[b3][:, 0:QT], t_rdT[:, (NIC - 1) * 128:NIC * 128], gu[:, (NIC - 1) * QT:NIC * QT],
            start=False, stop=True, skip_group_check=True),
            reads=[res("rdT"), rgu], writes=[pr[b3]])
        Sc.op("DVE", lambda v, rds=rds, wb=wb: v.tensor_tensor(rds[:], pbank[b3][:, 0:QT], wb[:], op=ALU.mult),
              reads=[pr[b3], rwb], writes=[rrds])
        Sc.op("PE", lambda t, cur=cur, rds=rds: t.matmul(
            pbank[b0][:, 0:QT], t_u3[cur][:, 256:384], rds[:],
            start=True, stop=True), reads=[ru3, rrds], writes=[pr[b0]])
        Sc.op("DVE", lambda v, e=e: v.tensor_copy(t_zd[:, e * QT:(e + 1) * QT], pbank[b0][:, 0:QT]),
              reads=[pr[b0]], writes=[rzd])
        yield

    def gen_shared(tt, par):
        b0, b1, b2, b3 = 4 * par, 4 * par + 1, 4 * par + 2, 4 * par + 3
        zg, zu, rds = t_zg[par], t_zu[par], t_rds[par]
        rzg, rzu, rrds = res(f"zg{par}"), res(f"zu{par}"), res(f"rds{par}")
        gs, gus = t_g[par], t_gu[par]
        rgs_, rgus = res(f"g{par}"), res(f"gu{par}")
        Sc.op("PE", lambda t, tt=tt: t.matmul(
            pbank[b0][:, 0:QT], t_su3[:, 0:128], t_rgs[:, tt * QT:(tt + 1) * QT],
            start=True, stop=True), reads=[res("su3"), res("rgs")], writes=[pr[b0]])
        Sc.op("ACT", lambda sc, zg=zg: sc.activation(zg[:], pbank[b0][:, 0:QT], AF.Copy),
              reads=[pr[b0]], writes=[rzg])
        Sc.op("PE", lambda t, tt=tt: t.matmul(
            pbank[b0][:, 0:QT], t_su3[:, 128:256], t_rus[:, tt * QT:(tt + 1) * QT],
            start=True, stop=True), reads=[res("su3"), res("rus")], writes=[pr[b0]])
        Sc.op("ACT", lambda sc, zu=zu: sc.activation(zu[:], pbank[b0][:, 0:QT], AF.Copy),
              reads=[pr[b0]], writes=[rzu])
        yield
        for c in range(NSC):
            gsl = c % 2
            Sc.op("PE", lambda t, c=c, zg=zg: t.matmul(
                pbank[b1][:, 0:QT], t_scg[:, c * 128:(c + 1) * 128], zg[:],
                start=True, stop=True), reads=[res("scg"), rzg], writes=[pr[b1]])
            Sc.op("PE", lambda t, c=c, zu=zu: t.matmul(
                pbank[b2][:, 0:QT], t_scu[:, c * 128:(c + 1) * 128], zu[:],
                start=True, stop=True), reads=[res("scu"), rzu], writes=[pr[b2]])
            Sc.op("ACT", lambda sc, gsl=gsl, gs=gs: sc.activation(
                gs[:, gsl * QT:(gsl + 1) * QT], pbank[b1][:, 0:QT], AF.Silu),
                reads=[pr[b1]], writes=[rgs_])
            if c > 0:
                Sc.op("PE", lambda t, c=c, pgsl=(c - 1) % 2, gus=gus: t.matmul(
                    pbank[b3][:, 0:QT], t_srd[:, (c - 1) * 128:c * 128], gus[:, pgsl * QT:(pgsl + 1) * QT],
                    start=(c == 1), stop=True, skip_group_check=(c != 1)),
                    reads=[res("srd"), rgus], writes=[pr[b3]])
            Sc.op("DVE", lambda v, gsl=gsl, gs=gs, gus=gus: v.tensor_tensor(
                gus[:, gsl * QT:(gsl + 1) * QT], gs[:, gsl * QT:(gsl + 1) * QT], pbank[b2][:, 0:QT], op=ALU.mult),
                reads=[rgs_, pr[b2]], writes=[rgus])
            yield
        Sc.op("PE", lambda t, pgsl=(NSC - 1) % 2, gus=gus: t.matmul(
            pbank[b3][:, 0:QT], t_srd[:, (NSC - 1) * 128:NSC * 128], gus[:, pgsl * QT:(pgsl + 1) * QT],
            start=False, stop=True, skip_group_check=True),
            reads=[res("srd"), rgus], writes=[pr[b3]])
        Sc.op("DVE", lambda v, rds=rds: v.tensor_copy(rds[:], pbank[b3][:, 0:QT]),
              reads=[pr[b3]], writes=[rrds])
        Sc.op("PE", lambda t, rds=rds: t.matmul(
            pbank[b0][:, 0:QT], t_su3[:, 256:384], rds[:],
            start=True, stop=True), reads=[res("su3"), rrds], writes=[pr[b0]])
        Sc.op("DVE", lambda v: v.tensor_copy(t_zd[:, E * QT:(E + 1) * QT], pbank[b0][:, 0:QT]),
              reads=[pr[b0]], writes=[rzd])
        yield

    for tt in range(NTT):
        for e0 in range(4):
            emit_eloads(e0, tt)
        units = [gen_expert(e, tt, e % 2) for e in range(E)]
        units.append(gen_shared(tt, 0))
        lanes = [None, None]
        qidx = 0
        uq = [[u for i, u in enumerate(units) if i % 2 == 0],
              [u for i, u in enumerate(units) if i % 2 == 1]]
        while any(uq) or any(l is not None for l in lanes):
            for p in range(2):
                if lanes[p] is None and uq[p]:
                    lanes[p] = uq[p].pop(0)
                if lanes[p] is not None:
                    try:
                        next(lanes[p])
                    except StopIteration:
                        lanes[p] = None

        for half in range(2):
            for ep in range(2):
                Sc.dma("POOL", f"lcd{ep}", lambda g, ep=ep: g.dma_start(
                    t_cd[ep][:], d_cd[ep * R:(ep + 1) * R, :]), reads=[res(f"scr_cd{ep}")], writes=[res(f"cd{ep}")])
            for e in range(E + 1):
                cur = e % 3
                rcd = res(f"cd{cur}")
                if e + 2 < E:
                    nc_ = (e + 2) % 3
                    Sc.dma("POOL", f"lcd{nc_}", lambda g, e=e, nc_=nc_: g.dma_start(
                        t_cd[nc_][:], d_cd[(e + 2) * R:(e + 3) * R, :]), reads=[res(f"scr_cd{e + 2}")], writes=[res(f"cd{nc_}")])
                for tci in range(2):
                    tc = 2 * half + tci
                    for ft in range(4):
                        bk = 4 * tci + ft
                        if e < E:
                            Sc.op("PE", lambda t, e=e, tc=tc, ft=ft, bk=bk, cur=cur: t.matmul(
                                pbank[bk][:, 0:QT],
                                t_zd[:, e * QT + tc * 128: e * QT + tc * 128 + 128],
                                t_cd[cur][:, ft * QT:(ft + 1) * QT],
                                start=(e == 0), stop=True, skip_group_check=(e != 0)),
                                reads=[rcd, rzd], writes=[pr[bk]])
                        else:
                            Sc.op("PE", lambda t, tc=tc, ft=ft, bk=bk: t.matmul(
                                pbank[bk][:, 0:QT],
                                t_zd[:, E * QT + tc * 128: E * QT + tc * 128 + 128],
                                t_scd[:, ft * QT:(ft + 1) * QT],
                                start=False, stop=True, skip_group_check=True),
                                reads=[res("scd"), rzd], writes=[pr[bk]])
            for tci in range(2):
                tc = 2 * half + tci
                ys = res(f"yst{tci}")
                for ft in range(4):
                    bk = 4 * tci + ft
                    Sc.op("DVE", lambda v, bk=bk, ft=ft, tci=tci: v.tensor_copy(
                        t_yst[tci][:, ft * QT:(ft + 1) * QT], pbank[bk][:, 0:QT]),
                        reads=[pr[bk]], writes=[ys])
                rb = tt * 4 + tc
                Sc.dma("POOL", "yacc", lambda g, tc=tc, tt=tt, tci=tci: g.dma_start(
                    dout[tt * QT + tc * 128: tt * QT + tc * 128 + 128, :],
                    t_yst[tci][:], accum_op=ALU.add),
                    reads=[ys], writes=[OUT_rb[rb]])

    close_scope(s4)
    do_taps(taps)
    Sc.final_wait("SP")
    Sc.emit()
    return nc



def kernel(**inputs):
    x = np.asarray(inputs["L0_attn_hidden_states"], dtype=np.float32)
    host = _prep_inputs(inputs)
    nc = build_program("full")
    in_maps = []
    for c in range(8):
        m = {"x": np.ascontiguousarray(x[c])}
        m.update(host)
        in_maps.append(m)
    res_ = run_bass_kernel_spmd(nc, in_maps, core_ids=list(range(8)))
    return np.stack([res_.results[c]["out"] for c in range(8)], axis=0)


# version 06
# speedup vs baseline: 1.0682x; 1.0682x over previous
import numpy as np
from contextlib import ExitStack

import concourse.bass as bass
import concourse.mybir as mybir
from concourse.bass_utils import run_bass_kernel_spmd

F32 = mybir.dt.float32
BF16 = mybir.dt.bfloat16
AF = mybir.ActivationFunctionType
ALU = mybir.AluOpType
AX = mybir.AxisListType

B, S, H = 8, 2048, 2048
NH, HD = 32, 64
NHC = 16
E, I, R = 32, 1408, 128
NIC = 11
SI, NSC = 2816, 22
QT, NQT = 512, 4
NTT = 4
MASKNEG = -60000.0
ATTN_HPS = None
DEBUG_MEMSET_OF = False
DEBUG_HP_BARRIER = False

ENGINES = ("PE", "ACT", "DVE", "POOL", "SP")
ENGMAP = {"PE": "tensor", "ACT": "scalar", "DVE": "vector", "POOL": "gpsimd", "SP": "sync"}


class Res:
    __slots__ = ("name", "writer", "readers")
    def __init__(self, name):
        self.name = name
        self.writer = None
        self.readers = []


class Sched:

    def __init__(self, nc):
        self.nc = nc
        self.sems = {e: nc.alloc_semaphore(f"q_{e}") for e in ENGINES}
        self.count = {e: 0 for e in ENGINES}
        self.ops = {e: [] for e in ENGINES}
        self.clock = {e: {} for e in ENGINES}
        self.postclock = {}
        self.dma_sems = {}
        self.slot_last_pc = {}

    def _resolve(self, eng, events):
        waits = []
        cl = self.clock[eng]
        for ev in events:
            if ev is None:
                continue
            s, v = ev
            pc = None
            if s == "SLOT":
                slot = v
                v = self.dma_sems[slot][1]
                s = slot
                pc = self.slot_last_pc.get(slot)
            else:
                pc = self.postclock.get(ev)
            if eng == "PE" and s == "PE":
                continue
            if cl.get(s, 0) >= v:
                continue
            waits.append((s, v))
            cl[s] = v
            if pc:
                for s2, v2 in pc.items():
                    if cl.get(s2, 0) < v2:
                        cl[s2] = v2
        return waits

    def _events(self, reads, writes):
        need = []
        for r in reads:
            need.append(r.writer)
        for w in writes:
            need.append(w.writer)
            need.extend(w.readers)
        return need

    def _commit(self, ev, reads, writes, eng):
        pc = dict(self.clock[eng])
        pc[ev[0]] = ev[1]
        self.postclock[ev] = pc
        for r in reads:
            r.readers.append(ev)
        for w in writes:
            w.writer = ev
            w.readers = []

    def op(self, eng, fn, reads=(), writes=()):
        waits = self._resolve(eng, self._events(reads, writes))
        self.count[eng] += 1
        ev = (eng, self.count[eng])
        self.ops[eng].append((waits, fn, eng, 1))
        self._commit(ev, reads, writes, eng)
        return ev

    def dma(self, eng, slot, fn, reads=(), writes=()):
        if slot not in self.dma_sems:
            self.dma_sems[slot] = [self.nc.alloc_semaphore(f"d_{slot}"), 0]
        waits = self._resolve(eng, self._events(reads, writes))
        self.dma_sems[slot][1] += 16
        ev = ("SLOT", slot)
        self.ops[eng].append((waits, fn, slot, 16))
        pc = dict(self.clock[eng])
        pc[slot] = self.dma_sems[slot][1]
        self.slot_last_pc[slot] = pc
        for r in reads:
            r.readers.append(ev)
        for w in writes:
            w.writer = ev
            w.readers = []
        return ev

    def hard_barrier(self):
        for eng in ENGINES:
            events = [(e2, self.count[e2]) for e2 in ENGINES if e2 != eng and self.count[e2] > 0]
            events += [("SLOT", s) for s in self.dma_sems]
            waits = self._resolve(eng, events)
            if waits:
                self.ops[eng].append((waits, None, None, 0))

    def final_wait(self, eng):
        events = [(name, c) for name, (h, c) in self.dma_sems.items()]
        waits = self._resolve(eng, events)
        if waits:
            self.ops[eng].append((waits, None, None, 0))

    def emit(self):
        handles = dict(self.sems)
        for name, (h, c) in self.dma_sems.items():
            handles[name] = h
        with self.nc.Block() as block:
            for eng in ENGINES:
                ops = self.ops[eng]
                if not ops:
                    continue
                def body(e, ops=ops):
                    for waits, fn, inc_key, inc_n in ops:
                        for (s, v) in waits:
                            e.wait_ge(handles[s], v)
                        if fn is not None:
                            inst = fn(e)
                            if inc_n:
                                inst.then_inc(handles[inc_key], inc_n)
                getattr(block, ENGMAP[eng])(body)



def _prep_inputs(ins):
    T = lambda a: np.ascontiguousarray(np.asarray(a).T.astype(np.float32, copy=False))
    d = {}
    d["WqT"] = T(ins["L0_attn_Wq"]); d["WkT"] = T(ins["L0_attn_Wk"])
    d["WvT"] = T(ins["L0_attn_Wv"]); d["WoT"] = T(ins["L0_attn_Wo"])
    d["gateT"] = T(ins["L0_ffn_gate_w"])
    d["RgT"] = T(ins["L0_ffn_Rg"]); d["RuT"] = T(ins["L0_ffn_Ru"]); d["RdT"] = T(ins["L0_ffn_Rd"])
    tp = lambda a: np.transpose(np.asarray(a).astype(np.float32, copy=False), (0, 2, 1))
    d["U3T"] = np.ascontiguousarray(np.concatenate(
        [tp(ins["L0_ffn_Ug"]), tp(ins["L0_ffn_Uu"]), tp(ins["L0_ffn_Ud"])], axis=2)).reshape(E * R, 3 * R)
    d["CguT"] = np.ascontiguousarray(np.concatenate(
        [tp(ins["L0_ffn_Cg"]), tp(ins["L0_ffn_Cu"])], axis=2)).reshape(E * R, 2 * I)
    d["CdT"] = np.ascontiguousarray(tp(ins["L0_ffn_Cd"])).reshape(E * R, H)
    d["sRgT"] = T(ins["L0_ffn_s_Rg"]); d["sRuT"] = T(ins["L0_ffn_s_Ru"])
    d["sU3T"] = np.ascontiguousarray(np.concatenate(
        [T(ins["L0_ffn_s_Ug"]), T(ins["L0_ffn_s_Uu"]), T(ins["L0_ffn_s_Ud"])], axis=1))
    d["sCgT"] = T(ins["L0_ffn_s_Cg"]); d["sCuT"] = T(ins["L0_ffn_s_Cu"])
    d["sRdT"] = T(ins["L0_ffn_s_Rd"]); d["sCdT"] = T(ins["L0_ffn_s_Cd"])

    tri = np.tril(np.ones((128, 128), np.float32))
    d["c_trineg"] = -tri
    d["c_ident"] = np.eye(128, dtype=np.float32)
    d["c_ones1"] = np.ones((1, 128), np.float32)
    d["c_negones"] = -np.ones((128, 1), np.float32)
    j = np.arange(128)[:, None]; i = np.arange(QT)[None, :]
    d["c_mneg"] = np.concatenate(
        [(j >= (i - 128 * Dd)).astype(np.float32) * MASKNEG for Dd in range(4)], axis=0)
    d["c_iota32"] = np.broadcast_to(np.arange(E, dtype=np.float32), (128, E)).copy()
    return d


DRAM_SHAPES = {
    "WqT": [H, H], "WkT": [H, H], "WvT": [H, H], "WoT": [H, H],
    "gateT": [H, E], "RgT": [H, R], "RuT": [H, R], "RdT": [I, R],
    "U3T": [E * R, 3 * R], "CguT": [E * R, 2 * I], "CdT": [E * R, H],
    "sRgT": [H, R], "sRuT": [H, R], "sU3T": [R, 3 * R],
    "sCgT": [R, SI], "sCuT": [R, SI], "sRdT": [SI, R], "sCdT": [R, H],
    "c_trineg": [128, 128], "c_ident": [128, 128], "c_ones1": [1, 128],
    "c_negones": [128, 1], "c_mneg": [4 * 128, QT], "c_iota32": [128, E],
}



def build_program(upto="full", taps=()):
    nc = bass.Bass("TRN2", target_bir_lowering=False, debug=False)
    Sc = Sched(nc)

    dx = nc.dram_tensor("x", [S, H], F32, kind="ExternalInput").ap()
    dout = nc.dram_tensor("out", [S, H], F32, kind="ExternalOutput").ap()
    dr = {k: nc.dram_tensor(k, shp, F32, kind="ExternalInput").ap() for k, shp in DRAM_SHAPES.items()}
    d_we = nc.dram_tensor("scr_we", [E, S], F32, kind="Internal").ap()
    d_cgu = nc.dram_tensor("scr_cgu", [E * R, 2 * I], BF16, kind="Internal").ap()
    d_cd = nc.dram_tensor("scr_cd", [E * R, H], BF16, kind="Internal").ap()
    d_u3 = nc.dram_tensor("scr_u3", [E * R, 3 * R], BF16, kind="Internal").ap()
    d_wo = nc.dram_tensor("scr_wo", [H, H], BF16, kind="Internal").ap()

    R_ = {}
    def res(name):
        if name not in R_:
            R_[name] = Res(name)
        return R_[name]

    sbufs = {}
    fence = []
    scope_res = {}
    def sb(name, shape, dtype, scope=None):
        if scope is None:
            t = nc.alloc_sbuf_tensor(name, shape, dtype)
        else:
            t = scope.enter_context(nc.sbuf_tensor(name, shape, dtype))
        sbufs[name] = t
        r = res(name)
        r.readers = list(fence)
        if scope is not None:
            scope_res.setdefault(id(scope), []).append(r)
        return t

    def close_scope(scope):
        for r in scope_res.pop(id(scope), []):
            if r.writer is not None:
                fence.append(r.writer)
            fence.extend(r.readers)
        scope.close()

    pbank = [nc.alloc_psum_tensor(f"pb{i}", [128, 512], F32) for i in range(8)]
    pr = [res(f"pb{i}") for i in range(8)]
    pb_bf = [p.ap().bitcast(BF16) for p in pbank]
    OUT_rb = [res(f"out_rb{i}") for i in range(16)]

    def ldc(dst_ap, src_ap, rname, slot="lconst"):
        Sc.dma("POOL", slot, lambda g, d=dst_ap, s=src_ap: g.dma_start(d, s), writes=[res(rname)])

    t_tri = sb("tri", [128, 128], BF16)
    t_idb = sb("idb", [128, 128], BF16)
    t_idf = sb("idf", [128, 128], F32)
    t_ones1 = sb("ones1", [1, 128], BF16)
    t_ones1f = sb("ones1f", [1, 128], F32)
    t_negones = sb("negones", [128, 1], BF16)
    t_mneg = sb("mneg", [128, 4 * QT], BF16)
    t_iota = sb("iota", [128, E], F32)
    t_rg = sb("rg", [128, S], BF16)
    t_ru = sb("ru", [128, S], BF16)
    t_rgs = sb("rgs", [128, S], BF16)
    t_rus = sb("rus", [128, S], BF16)
    ldc(t_tri[:], dr["c_trineg"], "tri")
    ldc(t_idb[:], dr["c_ident"], "idb")
    ldc(t_idf[:], dr["c_ident"], "idf")
    ldc(t_ones1[:], dr["c_ones1"], "ones1")
    ldc(t_ones1f[:], dr["c_ones1"], "ones1f")
    ldc(t_negones[:], dr["c_negones"], "negones")
    for Dd in range(4):
        ldc(t_mneg[:, Dd * QT:(Dd + 1) * QT], dr["c_mneg"][Dd * 128:(Dd + 1) * 128, :], "mneg")
    ldc(t_iota[:], dr["c_iota32"], "iota")

    for e in range(E):
        Sc.dma("POOL", "scast", lambda g, e=e: g.dma_start(d_cgu[e * R:(e + 1) * R, :], dr["CguT"][e * R:(e + 1) * R, :]),
               writes=[res(f"scr_cgu{e}")])
        Sc.dma("POOL", "scast", lambda g, e=e: g.dma_start(d_cd[e * R:(e + 1) * R, :], dr["CdT"][e * R:(e + 1) * R, :]),
               writes=[res(f"scr_cd{e}")])
        Sc.dma("POOL", "scast", lambda g, e=e: g.dma_start(d_u3[e * R:(e + 1) * R, :], dr["U3T"][e * R:(e + 1) * R, :]),
               writes=[res(f"scr_u3{e}")])
    for hc in range(NHC):
        Sc.dma("POOL", "scast", lambda g, hc=hc: g.dma_start(
            d_wo[hc * 128:(hc + 1) * 128, :], dr["WoT"][hc * 128:(hc + 1) * 128, :]),
            writes=[res(f"scr_wo{hc}")])

    s_oF = ExitStack()
    t_oF = sb("oF", [128, NHC * S], BF16, s_oF)
    if DEBUG_MEMSET_OF:
        Sc.op("DVE", lambda v: v.memset(t_oF[:], 0.0), writes=[res("oF")])

    s1 = ExitStack()
    t_xnF = sb("xnF", [128, NHC * S], BF16, s1)

    s1a = ExitStack()
    t_x = [sb(f"x{i}", [128, H], F32, s1a) for i in range(3)]
    t_xn = [sb(f"xn{i}", [128, H], BF16, s1a) for i in range(2)]
    t_stat2 = [sb(f"stat{i}", [128, 8], F32, s1a) for i in range(2)]
    t_xsq2 = [sb(f"xsq{i}", [128, H], F32, s1a) for i in range(2)]
    rxnF = res("xnF")

    for tt in range(16):
        t_stat = t_stat2[tt % 2]; t_xsq = t_xsq2[tt % 2]
        rstat, rxsq = res(f"stat{tt % 2}"), res(f"xsq{tt % 2}")
        xb = t_x[tt % 3]; rxb = res(f"x{tt % 3}")
        Sc.dma("SP", f"lx{tt % 3}", lambda s, d=xb, r0=tt * 128: s.dma_start(d[:], dx[r0:r0 + 128, :]), writes=[rxb])
        st = t_stat; xq = t_xsq
        Sc.op("DVE", lambda v, xb=xb, st=st: v.tensor_reduce(st[:, 0:1], xb[:], axis=AX.X, op=ALU.add),
              reads=[rxb], writes=[rstat])
        Sc.op("DVE", lambda v, xb=xb, xq=xq: v.tensor_tensor(xq[:], xb[:], xb[:], op=ALU.mult),
              reads=[rxb], writes=[rxsq])
        Sc.op("DVE", lambda v, st=st, xq=xq: v.tensor_reduce(st[:, 1:2], xq[:], axis=AX.X, op=ALU.add),
              reads=[rxsq], writes=[rstat])
        Sc.op("DVE", lambda v, st=st: v.tensor_scalar(st[:, 2:4], st[:, 0:2], 1.0 / H, None, op0=ALU.mult),
              reads=[rstat], writes=[rstat])
        Sc.op("DVE", lambda v, st=st: v.tensor_tensor(st[:, 4:5], st[:, 2:3], st[:, 2:3], op=ALU.mult),
              reads=[rstat], writes=[rstat])
        Sc.op("DVE", lambda v, st=st: v.tensor_tensor(st[:, 5:6], st[:, 3:4], st[:, 4:5], op=ALU.subtract),
              reads=[rstat], writes=[rstat])
        Sc.op("DVE", lambda v, st=st: v.tensor_scalar(st[:, 5:6], st[:, 5:6], 1e-5, None, op0=ALU.add),
              reads=[rstat], writes=[rstat])
        Sc.op("ACT", lambda sc, st=st: sc.activation(st[:, 6:7], st[:, 5:6], AF.Ln),
              reads=[rstat], writes=[rstat])
        Sc.op("ACT", lambda sc, st=st: sc.activation(st[:, 7:8], st[:, 6:7], AF.Exp, scale=-0.5),
              reads=[rstat], writes=[rstat])
        xnb = t_xn[tt % 2]; rxn = res(f"xn{tt % 2}")
        Sc.op("DVE", lambda v, xb=xb, xnb=xnb, st=st: v.tensor_scalar(
            xnb[:], xb[:], st[:, 2:3], st[:, 7:8], op0=ALU.subtract, op1=ALU.mult),
            reads=[rxb, rstat], writes=[rxn])
        for hc in range(NHC):
            bk = hc % 2
            Sc.op("PE", lambda t, xnb=xnb, hc=hc, bk=bk: t.transpose(
                pb_bf[bk][:, 0:128], xnb[:, hc * 128:(hc + 1) * 128], t_idb[:]),
                reads=[rxn, res("idb")], writes=[pr[bk]])
            Sc.op("DVE", lambda v, hc=hc, tt=tt, bk=bk: v.tensor_copy(
                t_xnF[:, hc * S + tt * 128: hc * S + tt * 128 + 128], pb_bf[bk][:, 0:128]),
                reads=[pr[bk]], writes=[rxnF])
    close_scope(s1a)

    s1b = ExitStack()
    t_rw = sb("rw", [128, NHC * 128], BF16, s1b)
    t_gateT = sb("gateTs", [128, NHC * E], BF16, s1b)
    t_lgF2 = [sb(f"lgF{i}", [32, QT], F32, s1b) for i in range(2)]
    t_lgT2 = [sb(f"lgT{i}", [128, E], F32, s1b) for i in range(2)]
    t_tw2 = [sb(f"tw{i}", [128, 8], F32, s1b) for i in range(2)]
    t_twi2 = [sb(f"twi{i}", [128, 8], mybir.dt.uint32, s1b) for i in range(2)]
    t_twif2 = [sb(f"twif{i}", [128, 8], F32, s1b) for i in range(2)]
    t_wnum2 = [sb(f"wnum{i}", [128, 8], F32, s1b) for i in range(2)]
    t_wden2 = [sb(f"wden{i}", [128, 2], F32, s1b) for i in range(2)]
    t_wacc2 = [sb(f"wacc{i}", [128, 4 * E], F32, s1b) for i in range(2)]
    t_weT = sb("weT", [32, S], F32, s1b)

    def proj128(dst, rdst, wname):
        rw = res("rw")
        for hc in range(NHC):
            ldc(t_rw[:, hc * 128:(hc + 1) * 128], dr[wname][hc * 128:(hc + 1) * 128, :], "rw", slot=f"lrw")
        for nt in range(NQT):
            bk = 4 + (nt % 2)
            for hc in range(NHC):
                Sc.op("PE", lambda t, hc=hc, nt=nt, bk=bk: t.matmul(
                    pbank[bk][:, 0:QT], t_rw[:, hc * 128:(hc + 1) * 128],
                    t_xnF[:, hc * S + nt * QT: hc * S + (nt + 1) * QT],
                    start=(hc == 0), stop=True, skip_group_check=(hc != 0)),
                    reads=[rw, rxnF], writes=[pr[bk]])
            Sc.op("DVE", lambda v, nt=nt, bk=bk: v.tensor_copy(
                dst[:, nt * QT:(nt + 1) * QT], pbank[bk][:, 0:QT]),
                reads=[pr[bk]], writes=[rdst])

    for dst, nm, wname in [(t_rg, "rg", "RgT"), (t_ru, "ru", "RuT"),
                           (t_rgs, "rgs", "sRgT"), (t_rus, "rus", "sRuT")]:
        proj128(dst, res(nm), wname)

    for hc in range(NHC):
        ldc(t_gateT[:, hc * E:(hc + 1) * E], dr["gateT"][hc * 128:(hc + 1) * 128, :], "gateTs", slot="lg")
    rweT = res("weT")
    for nt in range(NQT):
        gp = nt % 2
        t_lgF = t_lgF2[gp]; rlgF = res(f"lgF{gp}")
        gbk = 6 + gp
        for hc in range(NHC):
            Sc.op("PE", lambda t, hc=hc, nt=nt, gbk=gbk: t.matmul(
                pbank[gbk][0:32, 0:QT], t_gateT[:, hc * E:(hc + 1) * E],
                t_xnF[:, hc * S + nt * QT: hc * S + (nt + 1) * QT],
                start=(hc == 0), stop=True, skip_group_check=(hc != 0)),
                reads=[res("gateTs"), rxnF], writes=[pr[gbk]])
        Sc.op("DVE", lambda v, t_lgF=t_lgF, gbk=gbk: v.tensor_copy(t_lgF[:], pbank[gbk][0:32, 0:QT]),
              reads=[pr[gbk]], writes=[rlgF])
        for tc in range(4):
            tok0 = nt * QT + tc * 128
            p = tc % 2
            t_lgT = t_lgT2[p]; t_tw = t_tw2[p]; t_twi = t_twi2[p]; t_twif = t_twif2[p]
            t_wnum = t_wnum2[p]; t_wden = t_wden2[p]; t_wacc = t_wacc2[p]
            rlgT, rtw, rtwi, rtwif = res(f"lgT{p}"), res(f"tw{p}"), res(f"twi{p}"), res(f"twif{p}")
            rwnum, rwden, rwacc = res(f"wnum{p}"), res(f"wden{p}"), res(f"wacc{p}")
            tbk = 4 + p
            Sc.op("PE", lambda t, tc=tc, t_lgF=t_lgF, tbk=tbk: t.transpose(
                pbank[tbk][0:128, 0:32], t_lgF[0:32, tc * 128:(tc + 1) * 128], t_idf[0:32, 0:32]),
                reads=[rlgF, res("idf")], writes=[pr[tbk]])
            Sc.op("DVE", lambda v, t_lgT=t_lgT, tbk=tbk: v.tensor_copy(t_lgT[:], pbank[tbk][0:128, 0:32]),
                  reads=[pr[tbk]], writes=[rlgT])
            Sc.op("DVE", lambda v, t_tw=t_tw, t_lgT=t_lgT: v.max(t_tw[:], t_lgT[:]),
                  reads=[rlgT], writes=[rtw])
            Sc.op("DVE", lambda v, t_twi=t_twi, t_tw=t_tw, t_lgT=t_lgT: v.max_index(t_twi[:], t_tw[:], t_lgT[:]),
                  reads=[rtw, rlgT], writes=[rtwi])
            Sc.op("DVE", lambda v, t_twif=t_twif, t_twi=t_twi: v.tensor_copy(t_twif[:], t_twi[:]),
                  reads=[rtwi], writes=[rtwif])
            Sc.op("ACT", lambda sc, t_wnum=t_wnum, t_tw=t_tw: sc.activation(t_wnum[:, 0:4], t_tw[:, 0:4], AF.Exp),
                  reads=[rtw], writes=[rwnum])
            Sc.op("DVE", lambda v, t_wden=t_wden, t_wnum=t_wnum: v.tensor_reduce(t_wden[:, 0:1], t_wnum[:, 0:4], axis=AX.X, op=ALU.add),
                  reads=[rwnum], writes=[rwden])
            Sc.op("DVE", lambda v, t_wden=t_wden: v.reciprocal(t_wden[:, 1:2], t_wden[:, 0:1]),
                  reads=[rwden], writes=[rwden])
            Sc.op("DVE", lambda v, t_wnum=t_wnum, t_wden=t_wden: v.tensor_scalar(t_wnum[:, 4:8], t_wnum[:, 0:4], t_wden[:, 1:2], None, op0=ALU.mult),
                  reads=[rwden, rwnum], writes=[rwnum])
            for k in range(4):
                Sc.op("DVE", lambda v, k=k, t_wacc=t_wacc, t_twif=t_twif, t_wnum=t_wnum: v.tensor_scalar(
                    t_wacc[:, k * E:(k + 1) * E], t_iota[:], t_twif[:, k:k + 1], t_wnum[:, 4 + k:5 + k],
                    op0=ALU.is_equal, op1=ALU.mult),
                    reads=[res("iota"), rtwif, rwnum], writes=[rwacc])
            Sc.op("DVE", lambda v, t_wacc=t_wacc: v.tensor_tensor(t_wacc[:, 0:E], t_wacc[:, 0:E], t_wacc[:, E:2 * E], op=ALU.add),
                  reads=[rwacc], writes=[rwacc])
            Sc.op("DVE", lambda v, t_wacc=t_wacc: v.tensor_tensor(t_wacc[:, 2 * E:3 * E], t_wacc[:, 2 * E:3 * E], t_wacc[:, 3 * E:4 * E], op=ALU.add),
                  reads=[rwacc], writes=[rwacc])
            Sc.op("DVE", lambda v, t_wacc=t_wacc: v.tensor_tensor(t_wacc[:, 0:E], t_wacc[:, 0:E], t_wacc[:, 2 * E:3 * E], op=ALU.add),
                  reads=[rwacc], writes=[rwacc])
            Sc.op("PE", lambda t, t_wacc=t_wacc, tbk=tbk: t.transpose(pbank[tbk][0:32, 0:128], t_wacc[:, 0:E], t_idf[:]),
                  reads=[rwacc, res("idf")], writes=[pr[tbk]])
            Sc.op("DVE", lambda v, tok0=tok0, tbk=tbk: v.tensor_copy(t_weT[:, tok0:tok0 + 128], pbank[tbk][0:32, 0:128]),
                  reads=[pr[tbk]], writes=[rweT])
    Sc.dma("SP", "swe", lambda s: s.dma_start(d_we, t_weT[:]), reads=[rweT], writes=[res("scr_we")])

    taps_out = {}
    def do_taps(names):
        for nm in names:
            if nm in sbufs and nm not in taps_out:
                t = sbufs[nm]
                shp = list(t.ap().shape)
                dt_ = nc.dram_tensor(f"dbg_{nm}", shp, t.ap().dtype, kind="ExternalOutput").ap()
                Sc.dma("SP", f"dbg_{nm}", lambda s, dt_=dt_, t=t: s.dma_start(dt_, t[:]), reads=[res(nm)])
                taps_out[nm] = True

    if upto == "phase0":
        do_taps(taps)
        close_scope(s1b)
        Sc.final_wait("SP")
        Sc.emit()
        return nc
    close_scope(s1b)

    s2 = ExitStack()
    t_wq = sb("wq", [128, NHC * 128], BF16, s2)
    t_wk = sb("wk", [128, NHC * 128], BF16, s2)
    t_wv = sb("wv", [128, NHC * 128], BF16, s2)
    t_q2 = [sb(f"q{i}", [128, S], BF16, s2) for i in range(2)]
    t_k2 = [sb(f"k{i}", [128, S], BF16, s2) for i in range(2)]
    t_vtok2 = [sb(f"vtok{i}", [128, NHC * 128], BF16, s2) for i in range(2)]
    t_vF = sb("vF", [128, S], BF16, s2)
    t_texp = [sb(f"texp{i}", [128, QT], F32, s2) for i in range(2)]
    t_sp = [sb(f"sp{i}", [128, QT], BF16, s2) for i in range(4)]
    t_att = [sb(f"att{i}", [128, QT], BF16, s2) for i in range(4)]
    t_lsb = [sb(f"lsb{i}", [1, QT], BF16, s2) for i in range(2)]

    ZB = {0: (0, 1), 1: (6, 7)}
    PO_ = {0: 3, 1: 5}
    PL = 2
    QKVB = (4, 4)

    def gen_qkv(hp):
        par = hp % 2
        t_q, t_k, t_vtok = t_q2[par], t_k2[par], t_vtok2[par]
        rq, rk, rvtok = res(f"q{par}"), res(f"k{par}"), res(f"vtok{par}")
        for wname, rn in [("WqT", "wq"), ("WkT", "wk"), ("WvT", "wv")]:
            tdst = {"wq": t_wq, "wk": t_wk, "wv": t_wv}[rn]
            for hc in range(NHC):
                Sc.dma("POOL", f"l{rn}", lambda g, hc=hc, wname=wname, tdst=tdst, hp=hp: g.dma_start(
                    tdst[:, hc * 128:(hc + 1) * 128],
                    dr[wname][hc * 128:(hc + 1) * 128, hp * 128:(hp + 1) * 128]),
                    writes=[res(rn)])
            yield
        for tdst, rn, rdst, scale, wsrc, rwn in [
                (t_q, "q", rq, 0.125, t_wq, "wq"),
                (t_k, "k", rk, None, t_wk, "wk"),
                (t_vF, "vF", res("vF"), None, t_wv, "wv")]:
            rwsrc = res(rwn)
            for nt in range(NQT):
                bk = 4
                for hc in range(NHC):
                    Sc.op("PE", lambda t, hc=hc, nt=nt, bk=bk, wsrc=wsrc: t.matmul(
                        pbank[bk][:, 0:QT], wsrc[:, hc * 128:(hc + 1) * 128],
                        t_xnF[:, hc * S + nt * QT: hc * S + (nt + 1) * QT],
                        start=(hc == 0), stop=True, skip_group_check=(hc != 0)),
                        reads=[rwsrc, rxnF], writes=[pr[bk]])
                    if hc % 4 == 3:
                        yield
                if scale is None:
                    Sc.op("DVE", lambda v, nt=nt, bk=bk, tdst=tdst: v.tensor_copy(
                        tdst[:, nt * QT:(nt + 1) * QT], pbank[bk][:, 0:QT]),
                        reads=[pr[bk]], writes=[rdst])
                else:
                    Sc.op("DVE", lambda v, nt=nt, bk=bk, tdst=tdst, scale=scale: v.tensor_scalar(
                        tdst[:, nt * QT:(nt + 1) * QT], pbank[bk][:, 0:QT], scale, None, op0=ALU.mult),
                        reads=[pr[bk]], writes=[rdst])
                yield
        for kc in range(NHC):
            bk = 4
            Sc.op("PE", lambda t, kc=kc, bk=bk: t.transpose(
                pb_bf[bk][:, 0:128], t_vF[:, kc * 128:(kc + 1) * 128], t_idb[:]),
                reads=[res("vF"), res("idb")], writes=[pr[bk]])
            Sc.op("DVE", lambda v, kc=kc, bk=bk, t_vtok=t_vtok: v.tensor_copy(
                t_vtok[:, kc * 128:(kc + 1) * 128], pb_bf[bk][:, 0:128]),
                reads=[pr[bk]], writes=[rvtok])
            if kc % 2 == 1:
                yield

    def gen_head(hp, hs):
        par = hp % 2
        zb = ZB[hs]
        PO = PO_[hs]
        p0 = 64 * hs
        t_q, t_k, t_vtok = t_q2[par], t_k2[par], t_vtok2[par]
        rq, rk, rvtok = res(f"q{par}"), res(f"k{par}"), res(f"vtok{par}")
        lsb = t_lsb[hs]
        rlsb = res(f"lsb{hs}")
        sbase = 2 * hs

        def c0_of(jb, qt):
            return 128 * max(jb - 4 * qt, 0)

        def emit_z(jb, qt):
            pz = zb[jb % 2]
            Dd = jb - 4 * qt
            c0 = c0_of(jb, qt)
            Sc.op("PE", lambda t, jb=jb, qt=qt, p0=p0, pz=pz, c0=c0, t_k=t_k, t_q=t_q: t.matmul(
                pbank[pz][:, c0:QT],
                t_k[p0:p0 + 64, jb * 128:(jb + 1) * 128],
                t_q[p0:p0 + 64, qt * QT + c0:(qt + 1) * QT],
                start=True, stop=True),
                reads=[rk, rq], writes=[pr[pz]])
            if Dd >= 0:
                Sc.op("PE", lambda t, Dd=Dd, pz=pz, c0=c0: t.matmul(
                    pbank[pz][:, c0:QT], t_idb[:],
                    t_mneg[:, Dd * QT + c0:(Dd + 1) * QT],
                    start=False, stop=True, skip_group_check=True),
                    reads=[res("idb"), res("mneg")], writes=[pr[pz]])

        def emit_att_o(pjb, qt, jbmax):
            ppz, psl = zb[pjb % 2], sbase + pjb % 2
            c0 = c0_of(pjb, qt)
            first = (pjb == jbmax)
            Sc.op("ACT", lambda sc, psl=psl, ppz=ppz, c0=c0: sc.activation(
                t_att[psl][:, c0:QT], pbank[ppz][:, c0:QT], AF.Exp),
                reads=[pr[ppz]], writes=[res(f"att{psl}")])
            oc0 = 0 if first else c0
            Sc.op("PE", lambda t, psl=psl, pjb=pjb, p0=p0, oc0=oc0, first=first, t_vtok=t_vtok, PO=PO: t.matmul(
                pbank[PO][0:64, oc0:QT],
                t_vtok[:, pjb * 128 + p0: pjb * 128 + p0 + 64],
                t_att[psl][:, oc0:QT],
                start=first, stop=True, skip_group_check=not first),
                reads=[rvtok, res(f"att{psl}")], writes=[pr[PO]])

        for qt in range(NQT):
            jbmax = 4 * qt + 3
            Sc.op("DVE", lambda v, lsb=lsb: v.memset(lsb[:], 0.0), writes=[rlsb])
            c0m = c0_of(jbmax, qt)
            if c0m > 0:
                fsl = sbase + jbmax % 2
                Sc.op("DVE", lambda v, fsl=fsl, c0m=c0m: v.memset(t_att[fsl][:, 0:c0m], 0.0),
                      writes=[res(f"att{fsl}")])
            emit_z(jbmax, qt)
            pend = None
            for jb in range(jbmax, -1, -1):
                pz = zb[jb % 2]
                sl = sbase + jb % 2
                tx = jb % 2
                c0 = c0_of(jb, qt)
                Sc.op("ACT", lambda sc, tx=tx, pz=pz, c0=c0: sc.activation(
                    t_texp[tx][:, c0:QT], pbank[pz][:, c0:QT], AF.Exp),
                    reads=[pr[pz]], writes=[res(f"texp{tx}")])
                Sc.op("ACT", lambda sc, sl=sl, tx=tx, c0=c0: sc.activation(
                    t_sp[sl][:, c0:QT], t_texp[tx][:, c0:QT], AF.Ln, bias=1.0),
                    reads=[res(f"texp{tx}")], writes=[res(f"sp{sl}")])
                yield
                if pend is not None:
                    emit_att_o(pend, qt, jbmax)
                if jb > 0:
                    emit_z(jb - 1, qt)
                yield
                Sc.op("PE", lambda t, sl=sl, pz=pz, c0=c0: t.matmul(
                    pbank[pz][:, c0:QT], t_tri[:], t_sp[sl][:, c0:QT],
                    start=False, stop=True, skip_group_check=True),
                    reads=[res("tri"), res(f"sp{sl}")], writes=[pr[pz]])
                if jb != jbmax:
                    Sc.op("PE", lambda t, pz=pz, lsb=lsb, c0=c0: t.matmul(
                        pbank[pz][:, c0:QT], t_ones1[:], lsb[:, c0:QT],
                        start=False, stop=True, skip_group_check=True),
                        reads=[res("ones1"), rlsb], writes=[pr[pz]])
                if jb != 0:
                    Sc.op("PE", lambda t, sl=sl, c0=c0: t.matmul(
                        pbank[PL][0:1, c0:QT], t_negones[:], t_sp[sl][:, c0:QT],
                        start=True, stop=True),
                        reads=[res("negones"), res(f"sp{sl}")], writes=[pr[PL]])
                    Sc.op("DVE", lambda v, lsb=lsb, c0=c0: v.tensor_tensor(
                        lsb[:, c0:QT], lsb[:, c0:QT], pbank[PL][0:1, c0:QT], op=ALU.add),
                        reads=[pr[PL], rlsb], writes=[rlsb])
                yield
                pend = jb
            emit_att_o(pend, qt, jbmax)
            Sc.op("DVE", lambda v, p0=p0, qt=qt, hp=hp, PO=PO: v.tensor_copy(
                t_oF[p0:p0 + 64, hp * S + qt * QT: hp * S + (qt + 1) * QT],
                pbank[PO][0:64, 0:QT]),
                reads=[pr[PO]], writes=[res("oF")])
            yield

    hps = list(range(NHC)) if ATTN_HPS is None else list(ATTN_HPS)
    qkv_units = [gen_qkv(hp) for hp in hps]
    for _ in qkv_units[0]:
        pass
    for idx, hp in enumerate(hps):
        nxt = qkv_units[idx + 1] if idx + 1 < len(hps) else None
        gens = [gen_head(hp, 0), gen_head(hp, 1)]
        step = 0
        while gens:
            for gh in list(gens):
                try:
                    next(gh)
                except StopIteration:
                    gens.remove(gh)
            step += 1
            if nxt is not None and step % 2 == 0:
                try:
                    next(nxt)
                except StopIteration:
                    nxt = None
        while nxt is not None:
            try:
                next(nxt)
            except StopIteration:
                nxt = None

    close_scope(s2)
    close_scope(s1)

    if upto == "attn":
        do_taps(taps)
        Sc.final_wait("SP")
        Sc.emit()
        return nc

    s3 = ExitStack()
    t_wo2 = [sb(f"wo{i}", [128, NHC * QT], BF16, s3) for i in range(2)]
    t_xp = [sb(f"xp{i}", [128, QT], F32, s3) for i in range(2)]
    t_outp = [sb(f"op{i}", [128, QT], F32, s3) for i in range(3)]
    roF = res("oF")
    for ft in range(4):
        wsl = ft % 2
        t_wo = t_wo2[wsl]
        rwo = res(f"wo{wsl}")
        for hc in range(NHC):
            Sc.dma("SP", f"lwo{wsl}", lambda s, hc=hc, ft=ft, t_wo=t_wo: s.dma_start(
                t_wo[:, hc * QT:(hc + 1) * QT],
                d_wo[hc * 128:(hc + 1) * 128, ft * QT:(ft + 1) * QT]),
                reads=[res(f"scr_wo{hc}")], writes=[rwo])
        for tc in range(16):
            bk = tc % 2
            for hc in range(NHC):
                Sc.op("PE", lambda t, hc=hc, tc=tc, ft=ft, bk=bk, t_wo=t_wo: t.matmul(
                    pbank[bk][:, 0:QT],
                    t_oF[:, hc * S + tc * 128: hc * S + tc * 128 + 128],
                    t_wo[:, hc * QT:(hc + 1) * QT],
                    start=(hc == 0), stop=True, skip_group_check=(hc != 0)),
                    reads=[roF, rwo], writes=[pr[bk]])
            xs = tc % 2
            rxp = res(f"xp{xs}")
            Sc.dma("SP", f"lxp{xs}", lambda s, tc=tc, ft=ft, xs=xs: s.dma_start(
                t_xp[xs][:], dx[tc * 128:(tc + 1) * 128, ft * QT:(ft + 1) * QT]),
                writes=[rxp])
            osl = tc % 3
            rop = res(f"op{osl}")
            Sc.op("DVE", lambda v, bk=bk, xs=xs, osl=osl: v.tensor_tensor(
                t_outp[osl][:], pbank[bk][:, 0:QT], t_xp[xs][:], op=ALU.add),
                reads=[pr[bk], rxp], writes=[rop])
            Sc.dma("SP", "sout", lambda s, tc=tc, ft=ft, osl=osl: s.dma_start(
                dout[tc * 128:(tc + 1) * 128, ft * QT:(ft + 1) * QT], t_outp[osl][:]),
                reads=[rop], writes=[OUT_rb[tc]])
    close_scope(s3)
    close_scope(s_oF)

    s4 = ExitStack()
    t_rdT = sb("rdT", [128, NIC * 128], BF16, s4)
    t_su3 = sb("su3", [128, 3 * R], BF16, s4)
    t_scg = sb("scg", [128, SI], BF16, s4)
    t_scu = sb("scu", [128, SI], BF16, s4)
    t_srd = sb("srd", [128, NSC * 128], BF16, s4)
    t_scd = sb("scd", [128, H], BF16, s4)
    for c in range(NIC):
        ldc(t_rdT[:, c * 128:(c + 1) * 128], dr["RdT"][c * 128:(c + 1) * 128, :], "rdT", slot="lrdT")
    ldc(t_su3[:], dr["sU3T"], "su3", slot="lsh")
    ldc(t_scg[:], dr["sCgT"], "scg", slot="lsh")
    ldc(t_scu[:], dr["sCuT"], "scu", slot="lsh")
    for c in range(NSC):
        ldc(t_srd[:, c * 128:(c + 1) * 128], dr["sRdT"][c * 128:(c + 1) * 128, :], "srd", slot="lsh")
    ldc(t_scd[:], dr["sCdT"], "scd", slot="lsh")

    t_cgu = [sb(f"cgu{i}", [128, 2 * I], BF16, s4) for i in range(4)]
    t_u3 = [sb(f"u3_{i}", [128, 3 * R], BF16, s4) for i in range(4)]
    t_cd = [sb(f"cd{i}", [128, H], BF16, s4) for i in range(3)]
    t_zg = [sb(f"zg{i}", [128, QT], BF16, s4) for i in range(2)]
    t_zu = [sb(f"zu{i}", [128, QT], BF16, s4) for i in range(2)]
    t_g = [sb(f"g{i}", [128, NIC * QT], BF16, s4) for i in range(2)]
    t_gu = [sb(f"gu{i}", [128, NIC * QT], BF16, s4) for i in range(2)]
    t_wb = [sb(f"wb{i}", [128, QT], BF16, s4) for i in range(2)]
    t_rds = [sb(f"rds{i}", [128, QT], BF16, s4) for i in range(2)]
    t_zd = sb("zd", [128, (E + 1) * QT], BF16, s4)
    t_werow = [sb(f"werow{i}", [1, QT], F32, s4) for i in range(4)]
    t_yst = [sb(f"yst{i}", [128, H], F32, s4) for i in range(2)]
    rzd = res("zd")

    def emit_eloads(e, tt):
        cur = e % 4
        Sc.dma("SP", f"lcgu{cur}", lambda gq, e=e, cur=cur: gq.dma_start(
            t_cgu[cur][:], d_cgu[e * R:(e + 1) * R, :]), reads=[res(f"scr_cgu{e}")], writes=[res(f"cgu{cur}")])
        Sc.dma("SP", f"lu3{cur}", lambda gq, e=e, cur=cur: gq.dma_start(
            t_u3[cur][:], d_u3[e * R:(e + 1) * R, :]), reads=[res(f"scr_u3{e}")], writes=[res(f"u3_{cur}")])
        Sc.dma("SP", f"lwer{cur}", lambda s, e=e, cur=cur, tt=tt: s.dma_start(
            t_werow[cur][:], d_we[e:e + 1, tt * QT:(tt + 1) * QT]),
            reads=[res("scr_we")], writes=[res(f"werow{cur}")])

    def gen_expert(e, tt, par):
        b0, b1, b2, b3 = 4 * par, 4 * par + 1, 4 * par + 2, 4 * par + 3
        zg, zu, g, gu, wb, rds = t_zg[par], t_zu[par], t_g[par], t_gu[par], t_wb[par], t_rds[par]
        rzg, rzu, rg_, rgu = res(f"zg{par}"), res(f"zu{par}"), res(f"g{par}"), res(f"gu{par}")
        rwb, rrds = res(f"wb{par}"), res(f"rds{par}")
        cur = e % 4
        rcgu, ru3 = res(f"cgu{cur}"), res(f"u3_{cur}")
        if e + 2 < E:
            emit_eloads(e + 2, tt)
        yield
        Sc.op("PE", lambda t, cur=cur, tt=tt: t.matmul(
            pbank[b0][:, 0:QT], t_u3[cur][:, 0:128], t_rg[:, tt * QT:(tt + 1) * QT],
            start=True, stop=True), reads=[ru3, res("rg")], writes=[pr[b0]])
        Sc.op("ACT", lambda sc, zg=zg: sc.activation(zg[:], pbank[b0][:, 0:QT], AF.Copy),
              reads=[pr[b0]], writes=[rzg])
        Sc.op("PE", lambda t, cur=cur, tt=tt: t.matmul(
            pbank[b0][:, 0:QT], t_u3[cur][:, 128:256], t_ru[:, tt * QT:(tt + 1) * QT],
            start=True, stop=True), reads=[ru3, res("ru")], writes=[pr[b0]])
        Sc.op("ACT", lambda sc, zu=zu: sc.activation(zu[:], pbank[b0][:, 0:QT], AF.Copy),
              reads=[pr[b0]], writes=[rzu])
        Sc.op("PE", lambda t, cur=cur: t.matmul(
            pbank[b0][:, 0:QT], t_ones1f[:], t_werow[cur][:],
            start=True, stop=True), reads=[res("ones1f"), res(f"werow{cur}")], writes=[pr[b0]])
        Sc.op("DVE", lambda v, wb=wb: v.tensor_copy(wb[:], pbank[b0][:, 0:QT]),
              reads=[pr[b0]], writes=[rwb])
        yield
        for c in range(NIC):
            Sc.op("PE", lambda t, c=c, cur=cur, zg=zg: t.matmul(
                pbank[b1][:, 0:QT], t_cgu[cur][:, c * 128:(c + 1) * 128], zg[:],
                start=True, stop=True), reads=[rcgu, rzg], writes=[pr[b1]])
            Sc.op("PE", lambda t, c=c, cur=cur, zu=zu: t.matmul(
                pbank[b2][:, 0:QT], t_cgu[cur][:, I + c * 128: I + (c + 1) * 128], zu[:],
                start=True, stop=True), reads=[rcgu, rzu], writes=[pr[b2]])
            Sc.op("ACT", lambda sc, c=c, g=g: sc.activation(
                g[:, c * QT:(c + 1) * QT], pbank[b1][:, 0:QT], AF.Silu),
                reads=[pr[b1]], writes=[rg_])
            if c > 0:
                Sc.op("PE", lambda t, c=c, gu=gu: t.matmul(
                    pbank[b3][:, 0:QT], t_rdT[:, (c - 1) * 128:c * 128], gu[:, (c - 1) * QT:c * QT],
                    start=(c == 1), stop=True, skip_group_check=(c != 1)),
                    reads=[res("rdT"), rgu], writes=[pr[b3]])
            Sc.op("DVE", lambda v, c=c, g=g, gu=gu: v.tensor_tensor(
                gu[:, c * QT:(c + 1) * QT], g[:, c * QT:(c + 1) * QT], pbank[b2][:, 0:QT],
                op=ALU.mult), reads=[rg_, pr[b2]], writes=[rgu])
            yield
        Sc.op("PE", lambda t, gu=gu: t.matmul(
            pbank[b3][:, 0:QT], t_rdT[:, (NIC - 1) * 128:NIC * 128], gu[:, (NIC - 1) * QT:NIC * QT],
            start=False, stop=True, skip_group_check=True),
            reads=[res("rdT"), rgu], writes=[pr[b3]])
        Sc.op("DVE", lambda v, rds=rds, wb=wb: v.tensor_tensor(rds[:], pbank[b3][:, 0:QT], wb[:], op=ALU.mult),
              reads=[pr[b3], rwb], writes=[rrds])
        Sc.op("PE", lambda t, cur=cur, rds=rds: t.matmul(
            pbank[b0][:, 0:QT], t_u3[cur][:, 256:384], rds[:],
            start=True, stop=True), reads=[ru3, rrds], writes=[pr[b0]])
        Sc.op("DVE", lambda v, e=e: v.tensor_copy(t_zd[:, e * QT:(e + 1) * QT], pbank[b0][:, 0:QT]),
              reads=[pr[b0]], writes=[rzd])
        yield

    def gen_shared(tt, par):
        b0, b1, b2, b3 = 4 * par, 4 * par + 1, 4 * par + 2, 4 * par + 3
        zg, zu, rds = t_zg[par], t_zu[par], t_rds[par]
        rzg, rzu, rrds = res(f"zg{par}"), res(f"zu{par}"), res(f"rds{par}")
        gs, gus = t_g[par], t_gu[par]
        rgs_, rgus = res(f"g{par}"), res(f"gu{par}")
        Sc.op("PE", lambda t, tt=tt: t.matmul(
            pbank[b0][:, 0:QT], t_su3[:, 0:128], t_rgs[:, tt * QT:(tt + 1) * QT],
            start=True, stop=True), reads=[res("su3"), res("rgs")], writes=[pr[b0]])
        Sc.op("ACT", lambda sc, zg=zg: sc.activation(zg[:], pbank[b0][:, 0:QT], AF.Copy),
              reads=[pr[b0]], writes=[rzg])
        Sc.op("PE", lambda t, tt=tt: t.matmul(
            pbank[b0][:, 0:QT], t_su3[:, 128:256], t_rus[:, tt * QT:(tt + 1) * QT],
            start=True, stop=True), reads=[res("su3"), res("rus")], writes=[pr[b0]])
        Sc.op("ACT", lambda sc, zu=zu: sc.activation(zu[:], pbank[b0][:, 0:QT], AF.Copy),
              reads=[pr[b0]], writes=[rzu])
        yield
        for c in range(NSC):
            gsl = c % 2
            Sc.op("PE", lambda t, c=c, zg=zg: t.matmul(
                pbank[b1][:, 0:QT], t_scg[:, c * 128:(c + 1) * 128], zg[:],
                start=True, stop=True), reads=[res("scg"), rzg], writes=[pr[b1]])
            Sc.op("PE", lambda t, c=c, zu=zu: t.matmul(
                pbank[b2][:, 0:QT], t_scu[:, c * 128:(c + 1) * 128], zu[:],
                start=True, stop=True), reads=[res("scu"), rzu], writes=[pr[b2]])
            Sc.op("ACT", lambda sc, gsl=gsl, gs=gs: sc.activation(
                gs[:, gsl * QT:(gsl + 1) * QT], pbank[b1][:, 0:QT], AF.Silu),
                reads=[pr[b1]], writes=[rgs_])
            if c > 0:
                Sc.op("PE", lambda t, c=c, pgsl=(c - 1) % 2, gus=gus: t.matmul(
                    pbank[b3][:, 0:QT], t_srd[:, (c - 1) * 128:c * 128], gus[:, pgsl * QT:(pgsl + 1) * QT],
                    start=(c == 1), stop=True, skip_group_check=(c != 1)),
                    reads=[res("srd"), rgus], writes=[pr[b3]])
            Sc.op("DVE", lambda v, gsl=gsl, gs=gs, gus=gus: v.tensor_tensor(
                gus[:, gsl * QT:(gsl + 1) * QT], gs[:, gsl * QT:(gsl + 1) * QT], pbank[b2][:, 0:QT], op=ALU.mult),
                reads=[rgs_, pr[b2]], writes=[rgus])
            yield
        Sc.op("PE", lambda t, pgsl=(NSC - 1) % 2, gus=gus: t.matmul(
            pbank[b3][:, 0:QT], t_srd[:, (NSC - 1) * 128:NSC * 128], gus[:, pgsl * QT:(pgsl + 1) * QT],
            start=False, stop=True, skip_group_check=True),
            reads=[res("srd"), rgus], writes=[pr[b3]])
        Sc.op("DVE", lambda v, rds=rds: v.tensor_copy(rds[:], pbank[b3][:, 0:QT]),
              reads=[pr[b3]], writes=[rrds])
        Sc.op("PE", lambda t, rds=rds: t.matmul(
            pbank[b0][:, 0:QT], t_su3[:, 256:384], rds[:],
            start=True, stop=True), reads=[res("su3"), rrds], writes=[pr[b0]])
        Sc.op("DVE", lambda v: v.tensor_copy(t_zd[:, E * QT:(E + 1) * QT], pbank[b0][:, 0:QT]),
              reads=[pr[b0]], writes=[rzd])
        yield

    for tt in range(NTT):
        for e0 in range(4):
            emit_eloads(e0, tt)
        units = [gen_expert(e, tt, e % 2) for e in range(E)]
        units.append(gen_shared(tt, 0))
        lanes = [None, None]
        qidx = 0
        uq = [[u for i, u in enumerate(units) if i % 2 == 0],
              [u for i, u in enumerate(units) if i % 2 == 1]]
        while any(uq) or any(l is not None for l in lanes):
            for p in range(2):
                if lanes[p] is None and uq[p]:
                    lanes[p] = uq[p].pop(0)
                if lanes[p] is not None:
                    try:
                        next(lanes[p])
                    except StopIteration:
                        lanes[p] = None

        for half in range(2):
            for ep in range(2):
                Sc.dma("SP", f"lcd{ep}", lambda g, ep=ep: g.dma_start(
                    t_cd[ep][:], d_cd[ep * R:(ep + 1) * R, :]), reads=[res(f"scr_cd{ep}")], writes=[res(f"cd{ep}")])
            for e in range(E + 1):
                cur = e % 3
                rcd = res(f"cd{cur}")
                if e + 2 < E:
                    nc_ = (e + 2) % 3
                    Sc.dma("SP", f"lcd{nc_}", lambda g, e=e, nc_=nc_: g.dma_start(
                        t_cd[nc_][:], d_cd[(e + 2) * R:(e + 3) * R, :]), reads=[res(f"scr_cd{e + 2}")], writes=[res(f"cd{nc_}")])
                for tci in range(2):
                    tc = 2 * half + tci
                    for ft in range(4):
                        bk = 4 * tci + ft
                        if e < E:
                            Sc.op("PE", lambda t, e=e, tc=tc, ft=ft, bk=bk, cur=cur: t.matmul(
                                pbank[bk][:, 0:QT],
                                t_zd[:, e * QT + tc * 128: e * QT + tc * 128 + 128],
                                t_cd[cur][:, ft * QT:(ft + 1) * QT],
                                start=(e == 0), stop=True, skip_group_check=(e != 0)),
                                reads=[rcd, rzd], writes=[pr[bk]])
                        else:
                            Sc.op("PE", lambda t, tc=tc, ft=ft, bk=bk: t.matmul(
                                pbank[bk][:, 0:QT],
                                t_zd[:, E * QT + tc * 128: E * QT + tc * 128 + 128],
                                t_scd[:, ft * QT:(ft + 1) * QT],
                                start=False, stop=True, skip_group_check=True),
                                reads=[res("scd"), rzd], writes=[pr[bk]])
            for tci in range(2):
                tc = 2 * half + tci
                ys = res(f"yst{tci}")
                for ft in range(4):
                    bk = 4 * tci + ft
                    Sc.op("DVE", lambda v, bk=bk, ft=ft, tci=tci: v.tensor_copy(
                        t_yst[tci][:, ft * QT:(ft + 1) * QT], pbank[bk][:, 0:QT]),
                        reads=[pr[bk]], writes=[ys])
                rb = tt * 4 + tc
                Sc.dma("POOL", "yacc", lambda g, tc=tc, tt=tt, tci=tci: g.dma_start(
                    dout[tt * QT + tc * 128: tt * QT + tc * 128 + 128, :],
                    t_yst[tci][:], accum_op=ALU.add),
                    reads=[ys], writes=[OUT_rb[rb]])

    close_scope(s4)
    do_taps(taps)
    Sc.final_wait("SP")
    Sc.emit()
    return nc



def kernel(**inputs):
    x = np.asarray(inputs["L0_attn_hidden_states"], dtype=np.float32)
    host = _prep_inputs(inputs)
    nc = build_program("full")
    in_maps = []
    for c in range(8):
        m = {"x": np.ascontiguousarray(x[c])}
        m.update(host)
        in_maps.append(m)
    res_ = run_bass_kernel_spmd(nc, in_maps, core_ids=list(range(8)))
    return np.stack([res_.results[c]["out"] for c in range(8)], axis=0)
